# version 37
# baseline (speedup 1.0000x reference)
import contextlib
import numpy as np
import ml_dtypes
import concourse.bass as bass
import concourse.mybir as mybir
from concourse.bass_utils import run_bass_kernel_spmd

F32 = mybir.dt.float32
BF16 = mybir.dt.bfloat16
AF = mybir.ActivationFunctionType
ALU = mybir.AluOpType
AXX = mybir.AxisListType.X

NCORES = 8
D = 1024
DT = D // 128
H = 16
DV = 64
FFN_H = 4096
EPS_LN = 1e-5
EPS_NORM = 1e-8

MAGIC23 = 12582912.0
MAGIC_K = 134217728.0 * (6755399441055744.0 - 12582912.0)
EPS27 = 7.450580596923828e-09


class Prog:

    ENGS = ("T", "V", "S", "P", "Q")
    DMAQ = ("P", "Q")
    NDS = 6

    def __init__(self, nc, stack):
        self.nc = nc
        self.sem = {e: stack.enter_context(nc.semaphore("sem" + e)) for e in self.ENGS}
        self.dsem = {q: [stack.enter_context(nc.semaphore("dma%s%d" % (q, i)))
                         for i in range(self.NDS)] for q in self.DMAQ}
        self.cnt = {e: 0 for e in self.ENGS}
        self.dcnt = {q: [0] * self.NDS for q in self.DMAQ}
        self.drr = {q: 0 for q in self.DMAQ}
        self.ops = {e: [] for e in self.ENGS}
        self.wm = {e: {} for e in self.ENGS}
        self.last_w = {}
        self.readers = {}
        self.pending = set()
        self.n_instr = 0

    def _wait(self, e, sem, c, sid):
        if c <= 0 or self.wm[e].get(sid, 0) >= c:
            return
        self.wm[e][sid] = c
        self.ops[e].append(lambda eng, sem=sem, c=c: eng.wait_ge(sem, c))
        self.n_instr += 1

    def _wait_cover(self, e, cover):
        kind = cover[0]
        if kind == "c":
            _, f, c = cover
            self._wait(e, self.sem[f], c, f)
        else:
            _, q, i, c = cover
            self._wait(e, self.dsem[q][i], c, (q, i))

    def op(self, e, fn, reads=(), writes=(), dma=False, keep=()):
        need = {}

        def _need(cov):
            if cov is None:
                return
            key = cov[:-1]
            if key not in need or need[key][-1] < cov[-1]:
                need[key] = cov
        for k in reads:
            _need(self.last_w.get(k))
        for k in writes:
            _need(self.last_w.get(k))
            for cov in self.readers.get(k, {}).values():
                _need(cov)
        for cov in sorted(need.values(), key=str):
            self._wait_cover(e, cov)
        if dma:
            assert e in self.DMAQ
            i = self.drr[e]
            self.drr[e] = (i + 1) % self.NDS
            prev = self.dcnt[e][i]
            if prev:
                self._wait(e, self.dsem[e][i], prev, (e, i))
            self.dcnt[e][i] = prev + 16
            cover = ("d", e, i, prev + 16)
            sem = self.dsem[e][i]

            def emit(eng, fn=fn, sem=sem):
                fn(eng).then_inc(sem, 16)
        else:
            self.cnt[e] += 1
            cover = ("c", e, self.cnt[e])
            sem = self.sem[e]

            def emit(eng, fn=fn, sem=sem):
                fn(eng).then_inc(sem, 1)

        self.ops[e].append(emit)
        self.n_instr += 1
        for k in reads:
            self.readers.setdefault(k, {})[(e, dma)] = cover
            if k not in keep:
                self.pending.discard(k)
        for k in writes:
            self.last_w[k] = cover
            self.readers[k] = {}
            if isinstance(k, tuple) and k and k[0] in ("pb", "tb"):
                self.pending.add(k)

    def barrier(self):
        covers = [("c", f, self.cnt[f]) for f in self.ENGS]
        covers += [("d", q, i, self.dcnt[q][i]) for q in self.DMAQ for i in range(self.NDS)]
        for e in self.ENGS:
            for cov in covers:
                if cov[-1] > 0:
                    self._wait_cover(e, cov)

    def drain(self):
        self.barrier()

    def finalize(self):
        with self.nc.Block() as block:
            block.tensor(lambda eng: [f(eng) for f in self.ops["T"]])
            block.vector(lambda eng: [f(eng) for f in self.ops["V"]])
            block.scalar(lambda eng: [f(eng) for f in self.ops["S"]])
            block.gpsimd(lambda eng: [f(eng) for f in self.ops["P"]])
            block.sync(lambda eng: [f(eng) for f in self.ops["Q"]])


def _uptri_mask_T():
    m = np.zeros((128, 128), dtype=np.float32)
    for k in range(128):
        m[k, k:] = 1.0
    return m.astype(ml_dtypes.bfloat16)


class Kern:
    def __init__(self, nc, prog, stack, S):
        self.nc = nc
        self.p = prog
        self.stack = stack
        self.S = S
        self.NT = S // 128
        sb = lambda name, shape, dt: stack.enter_context(nc.sbuf_tensor(name, shape, dt))
        ps = lambda name, shape, dt: stack.enter_context(nc.psum_tensor(name, shape, dt))
        NT = self.NT
        self.r = sb("r", [128, NT * 1024], BF16)
        self.ident = sb("ident", [128, 128], BF16)
        self.maskT = sb("maskT", [128, 128], BF16)
        self.magic = sb("magic", [128, 1], F32)
        self.ones_bf = sb("ones_bf", [128, 128], BF16)
        self.onerow = sb("onerow", [1, 128], F32)
        self.identf = sb("identf", [128, 128], F32)
        self.pbd = [ps("pbd%d" % i, [128, 1024], F32) for i in range(2)]
        self.pbs = [ps("pbs%d" % i, [128, 512], F32) for i in range(2)]
        self.pb = [self.pbd[0][:, 0:512], self.pbd[0][:, 512:1024],
                   self.pbd[1][:, 0:512], self.pbd[1][:, 512:1024],
                   self.pbs[0][:, :], self.pbs[1][:, :]]
        self.dummy_ps = self.pbs[1]
        self._pbd_rr = 0
        self.wch = [sb("wchp%d" % i, [128, 512], F32) for i in range(2)]
        self.swi = {t: sb("swi" + t, [128, 1], F32) for t in ("a", "b")}
        self.swv = {t: sb("sw" + t, [128, 1], F32) for t in ("a", "b")}
        self.wacc = sb("wacc", [128, 4], F32)
        self.onef = sb("onef", [128, 1], F32)
        self.wq_dram = {}
        self.tb = [ps("tb%d" % i, [128, 1024], BF16) for i in range(2)]
        self._tb_rr = 0
        self._pb_rr = 0
        self._nid = 0
        self.ident_dram = nc.inline_tensor(np.eye(128, dtype=ml_dtypes.bfloat16), "identc")
        self.identf_dram = nc.inline_tensor(np.eye(128, dtype=np.float32), "identfc")
        self.mask_dram = nc.inline_tensor(_uptri_mask_T(), "maskc")

    def init_consts(self):
        p = self.p
        p.op("Q", lambda e: e.dma_start(self.ident[:], self.ident_dram.ap()),
             writes=[("ident",)], dma=True)
        p.op("Q", lambda e: e.dma_start(self.maskT[:], self.mask_dram.ap()),
             writes=[("maskT",)], dma=True)
        p.op("V", lambda e: e.memset(self.ones_bf[:], 1.0), writes=[("ones_bf",)])
        p.op("V", lambda e: e.memset(self.onerow[:], 1.0), writes=[("onerow",)])
        p.op("Q", lambda e: e.dma_start(self.identf[:], self.identf_dram.ap()),
             writes=[("identf",)], dma=True)
        p.op("V", lambda e: e.memset(self.magic[:], 1.0), writes=[("magic",)])
        p.op("V", lambda e: e.tensor_scalar(self.magic[:], self.magic[:], EPS27, -1.0,
                                            ALU.add, ALU.add),
             reads=[("magic",)], writes=[("magic",)])
        p.op("V", lambda e: e.tensor_scalar(self.magic[:], self.magic[:], MAGIC_K, MAGIC23,
                                            ALU.mult, ALU.add),
             reads=[("magic",)], writes=[("magic",)])

    def _sb(self, st, name, shape, dt):
        self._nid += 1
        return st.enter_context(self.nc.sbuf_tensor("sb%d_%s" % (self._nid, name), shape, dt))

    def tbank(self):
        for _ in range(2):
            b = self._tb_rr
            self._tb_rr = (self._tb_rr + 1) % 2
            if ("tb", b) not in self.p.pending:
                return self.tb[b], ("tb", b)
        raise RuntimeError("both transpose banks pending")

    def warm(self):
        self.p.op("T", lambda e: e.matmul(
            self.dummy_ps[0:64, 0:64], self.ones_bf[0:1, 0:64],
            self.ones_bf[0:1, 0:64], start=True, stop=True),
            reads=[("ones_bf",)], writes=[])

    def bank_pair(self):
        for _ in range(2):
            i = self._pbd_rr
            self._pbd_rr = (self._pbd_rr + 1) % 2
            if (("pb", 2 * i) not in self.p.pending
                    and ("pb", 2 * i + 1) not in self.p.pending):
                return self.pbd[i], ("pb", 2 * i), ("pb", 2 * i + 1)
        raise RuntimeError("both psum pairs pending")

    def bank(self):
        n = len(self.pb)
        for _ in range(n):
            b = self._pb_rr
            self._pb_rr = (self._pb_rr + 1) % n
            if ("pb", b) not in self.p.pending:
                return self.pb[b], ("pb", b)
        raise RuntimeError("all psum banks pending (producer emitted with no "
                           "consumer before next producer)")

    def load_r(self, x_dram, s):
        p, NT = self.p, self.NT
        for t0 in range(0, NT, 2):
            src = x_dram[s, t0 * 128:(t0 + 2) * 128, :].rearrange(
                "(t p) c -> p t c", p=128)
            dst = self.r[:, t0 * 1024:(t0 + 2) * 1024].rearrange(
                "p (t c) -> p t c", c=1024)
            p.op("P", lambda e, src=src, dst=dst: e.dma_start(dst, src),
                 writes=[("r", t0), ("r", t0 + 1)], dma=True)

    def store_r(self, out_dram, s):
        p, NT = self.p, self.NT
        dst = out_dram[s].rearrange("(t p) c -> p t c", p=128)
        src = self.r[:].rearrange("p (t c) -> p t c", c=1024)
        p.op("P", lambda e: e.dma_start(dst, src),
             reads=[("r", t) for t in range(NT)], dma=True)

    def transpose_r(self, rT):
        p, NT, S = self.p, self.NT, self.S
        for dt in range(DT):
            for t in range(NT):
                i = dt * NT + t
                tb, tkey = self.tbank()
                src = self.r[:, t * 1024 + dt * 128: t * 1024 + (dt + 1) * 128]
                p.op("T", lambda e, src=src, tb=tb: e.transpose(tb[:, 0:128], src, self.ident[:]),
                     reads=[("r", t), ("ident",)], writes=[tkey])
                dstc = dt * S + t * 128
                if t % 2 == 0:
                    p.op("V", lambda e, tb=tb, dstc=dstc, rT=rT: e.tensor_copy(
                        rT[:, dstc:dstc + 128], tb[:, 0:128]),
                        reads=[tkey], writes=[("rT", dt, t)])
                else:
                    p.op("S", lambda e, tb=tb, dstc=dstc, rT=rT: e.copy(
                        rT[:, dstc:dstc + 128], tb[:, 0:128]),
                        reads=[tkey], writes=[("rT", dt, t)])

    def ln_tile(self, rsum_key, rsum_ap, sq, s1, s2, out_ap, out_keys, n_feat, eps,
                kq=("lnsq",)):
        p = self.p
        k1, k2 = ("lns1",), ("lns2",)
        nch = n_feat // 512
        for ch in range(nch):
            p.op("V", lambda e, ch=ch: e.bn_stats(
                sq[:, ch * 6:(ch + 1) * 6], rsum_ap[:, ch * 512:(ch + 1) * 512]),
                 reads=[rsum_key], writes=[kq])
        p.op("V", lambda e: e.bn_aggr(s1[:, 0:2], sq[:, 0:6 * nch]),
             reads=[kq], writes=[k1])
        p.op("V", lambda e: e.tensor_scalar_add(s2[:, 0:1], s1[:, 1:2], eps),
             reads=[k1], writes=[k2])
        p.op("S", lambda e: e.activation(s2[:, 1:2], s2[:, 0:1], AF.Sqrt),
             reads=[k2], writes=[k2])
        p.op("V", lambda e: e.reciprocal(s2[:, 2:3], s2[:, 1:2]),
             reads=[k2], writes=[k2])
        p.op("V", lambda e: e.tensor_scalar(out_ap, rsum_ap, s1[:, 0:1], s2[:, 2:3],
                                            ALU.subtract, ALU.mult),
             reads=[rsum_key, k1, k2], writes=out_keys)


    def l0_attn(self, w_q, w_kv, w_o, k_weights):
        nc, p = self.nc, self.p
        S, NT = self.S, self.NT
        with contextlib.ExitStack() as st:
            sb = lambda name, shape, dt: self._sb(st, name, shape, dt)
            rT = sb("rT", [128, DT * S], BF16)
            self.transpose_r(rT)
            qTa = [sb("qTa", [65, 2 * S], BF16) for _ in range(2)]
            kTa = [sb("kTa", [65, 2 * S], BF16) for _ in range(2)]
            kvtok = [sb("kvtok", [128, NT * 2 * 65], BF16) for _ in range(2)]
            wbuf = [sb("wbuf", [128, 2 * DT * 128], BF16) for _ in range(2)]
            kw = [sb("kw", [128, 2], F32) for _ in range(2)]
            kvs = [sb("kvs", [128, S], BF16) for _ in range(2)]
            mneg = [sb("mneg", [128, NT], F32) for _ in range(2)]
            mnegb = [sb("mnegb", [128, NT], BF16) for _ in range(2)]
            mrow = [sb("mrow", [NT, 128], BF16) for _ in range(2)]
            eT = sb("eT", [128, NT * 512], BF16)
            yT = sb("yT", [128, DT * S], BF16)
            rec = sb("rec", [1, 512], F32)
            recb = sb("recb", [1, 512], BF16)
            bcd = sb("bcd", [64, 512], F32)
            wo = sb("wo", [128, DT * 1024], BF16)
            rsum = sb("rsum", [128, 1024], F32)
            sq = sb("sq", [128, 1024], F32)
            s1 = sb("s1", [128, 4], F32)
            s2 = sb("s2", [128, 4], F32)

            p.op("P", lambda e: e.dma_start(
                wo[:].rearrange("p (dt c) -> p dt c", c=1024),
                w_o.rearrange("(dt p) c -> p dt c", p=128)),
                writes=[("wo",)], dma=True)
            for pr in range(2):
                p.op("V", lambda e, pr=pr: e.memset(kTa[pr][64:65, :], 1.0),
                     writes=[("kTa_ones", pr)])
                p.op("V", lambda e, pr=pr: e.memset(
                    kvtok[pr][:].rearrange("p (t h c) -> p t h c", h=2, c=65)[:, :, :, 64:65],
                    1.0),
                    writes=[("kvtok_ones", pr)])

            def prep(g):
                pr = g % 2
                qT_, kT_, kv_, wb_, kw_, kvs_ = (qTa[pr], kTa[pr], kvtok[pr],
                                                 wbuf[pr], kw[pr], kvs[pr])
                p.op("P", lambda e: e.dma_start(
                    wb_[:, 0:DT * 128].rearrange("p (dt c) -> p dt c", c=128),
                    w_q[:, g * 128:(g + 1) * 128].rearrange("(dt p) c -> p dt c", p=128)),
                    writes=[("wbufq", pr)], dma=True)
                yield
                p.op("P", lambda e: e.dma_start(
                    wb_[:, DT * 128:].rearrange("p (dt c) -> p dt c", c=128),
                    w_kv[:, g * 128:(g + 1) * 128].rearrange("(dt p) c -> p dt c", p=128)),
                    writes=[("wbufk", pr)], dma=True)
                yield
                p.op("P", lambda e: e.dma_start(
                    kw_[:, 0:1], k_weights[g * 128:(g + 1) * 128, None]),
                    writes=[("kw", pr)], dma=True)
                yield
                p.op("V", lambda e: e.tensor_scalar_add(kw_[:, 1:2], kw_[:, 0:1], 1.0),
                     reads=[("kw", pr)], writes=[("kw1", pr)])
                yield
                for which in range(2):
                    wk = ("wbufq", pr) if which == 0 else ("wbufk", pr)
                    for c in range(S // 512):
                        bank, bkey = self.bank()
                        rds = [wk] + [("rT", dt, tt) for dt in range(DT)
                                      for tt in range(c * 4, c * 4 + 4)]

                        def mm(e, which=which, c=c, bank=bank, wb_=wb_):
                            for dt in range(DT):
                                ins = e.matmul(
                                    bank[:],
                                    wb_[:, which * DT * 128 + dt * 128:
                                        which * DT * 128 + (dt + 1) * 128],
                                    rT[:, dt * S + c * 512: dt * S + (c + 1) * 512],
                                    start=(dt == 0), stop=(dt == DT - 1))
                            return ins
                        p.op("T", mm, reads=rds, writes=[bkey])
                        yield
                        if which == 0:
                            for hh in range(2):
                                p.op("V", lambda e, hh=hh, c=c, bank=bank, qT_=qT_:
                                     e.tensor_scalar_mul(
                                         qT_[0:64, hh * S + c * 512: hh * S + (c + 1) * 512],
                                         bank[hh * 64:hh * 64 + 64, :], 1.0 / 32.0),
                                     reads=[bkey], writes=[("qTa", pr, hh, c)])
                                yield
                        else:
                            p.op("V", lambda e, c=c, bank=bank, kvs_=kvs_: e.tensor_copy(
                                kvs_[:, c * 512:(c + 1) * 512], bank[:]),
                                reads=[bkey], writes=[("kvs", pr, c)])
                            yield
                            for hh in range(2):
                                p.op("V", lambda e, hh=hh, c=c, kT_=kT_, kvs_=kvs_, kw_=kw_:
                                     e.tensor_scalar(
                                         kT_[0:64, hh * S + c * 512: hh * S + (c + 1) * 512],
                                         kvs_[hh * 64:hh * 64 + 64, c * 512:(c + 1) * 512],
                                         kw_[hh * 64:hh * 64 + 64, 1:2], None, ALU.mult),
                                     reads=[("kvs", pr, c), ("kw1", pr)],
                                     writes=[("kTa", pr, hh, c)])
                                yield
                for hh in range(2):
                    for t in range(NT):
                        tb, tkey = self.tbank()
                        p.op("T", lambda e, hh=hh, t=t, tb=tb, kvs_=kvs_: e.transpose(
                            tb[:, 0:64], kvs_[hh * 64:hh * 64 + 64, t * 128:(t + 1) * 128],
                            self.ident[hh * 64:hh * 64 + 64, hh * 64:hh * 64 + 64]),
                            reads=[("kvs", pr, t // 4), ("ident",)], writes=[tkey])
                        yield
                        dst = kv_[:, (t * 2 + hh) * 65: (t * 2 + hh) * 65 + 64]
                        p.op("V", lambda e, dst=dst, tb=tb: e.tensor_copy(dst, tb[:, 0:64]),
                             reads=[tkey], writes=[("kvtok", pr, hh, t)])
                        yield
                for hh in range(2):
                    for qt in range(NT):
                        c = qt // 4
                        cols = min(512, (qt + 1) * 128 - c * 512)
                        bank, bkey = self.bank()
                        p.op("T", lambda e, hh=hh, qt=qt, c=c, cols=cols, bank=bank,
                             qT_=qT_, kT_=kT_:
                             e.matmul(bank[:, 0:cols],
                                      qT_[0:64, hh * S + qt * 128:hh * S + (qt + 1) * 128],
                                      kT_[0:64, hh * S + c * 512:hh * S + c * 512 + cols],
                                      start=True, stop=True),
                             reads=[("qTa", pr, hh, qt // 4), ("kTa", pr, hh, c)],
                             writes=[bkey])
                        yield
                        p.op("V", lambda e, qt=qt, cols=cols, bank=bank, mn=mneg[pr]:
                             e.reduce_max(mn[:, qt:qt + 1], bank[:, 0:cols],
                                          axis=AXX, negate=True),
                             reads=[bkey], writes=[("mneg", pr, qt)])
                        yield
                    p.op("V", lambda e, mn=mneg[pr], mb=mnegb[pr]: e.tensor_copy(mb[:], mn[:]),
                         reads=[("mneg", pr, qt2) for qt2 in range(NT)],
                         writes=[("mnegb", pr)])
                    yield
                    mtb, mtkey = self.tbank()
                    p.op("T", lambda e, mb=mnegb[pr], mtb=mtb: e.transpose(
                        mtb[0:NT, 0:128], mb[:], self.ident[:]),
                        reads=[("mnegb", pr), ("ident",)], writes=[mtkey])
                    yield
                    p.op("V", lambda e, mr=mrow[pr], mtb=mtb: e.tensor_copy(
                        mr[:], mtb[0:NT, 0:128]),
                        reads=[mtkey], writes=[("mrow", pr)])
                    yield
                    p.op("P", lambda e, hh=hh, qT_=qT_, mr=mrow[pr]: e.dma_start(
                        qT_[64:65, hh * S:(hh + 1) * S].rearrange("o (t c) -> o t c", c=128),
                        mr[:, :]),
                        reads=[("mrow", pr)], writes=[("qTa64", pr, hh)], dma=True)
                    yield

            def c3(g):
                pr = g % 2
                qT_, kT_, kv_ = qTa[pr], kTa[pr], kvtok[pr]
                for hh in range(2):
                    h = 2 * g + hh
                    for c in range(S // 512):
                        ktmax = min(NT, c * 4 + 4)
                        ekeys = []
                        kt = 0
                        while kt < ktmax:
                            q0 = max(0, kt * 128 - c * 512)
                            bank, bkey = self.bank()
                            p.op("T", lambda e, hh=hh, c=c, kt=kt, q0=q0, bank=bank,
                                 qT_=qT_, kT_=kT_:
                                 e.matmul(bank[:, q0:512],
                                          kT_[:, hh * S + kt * 128:hh * S + (kt + 1) * 128],
                                          qT_[:, hh * S + c * 512 + q0:hh * S + (c + 1) * 512],
                                          start=True, stop=True),
                                 reads=[("kTa", pr, hh, kt // 4), ("kTa_ones", pr),
                                        ("qTa", pr, hh, c), ("qTa64", pr, hh)],
                                 writes=[bkey])
                            yield
                            ek = ("eT", kt)
                            p.op("S", lambda e, kt=kt, q0=q0, bank=bank: e.activation(
                                eT[:, kt * 512 + q0:(kt + 1) * 512], bank[:, q0:512], AF.Exp),
                                reads=[bkey], writes=[ek])
                            yield
                            if kt * 128 >= c * 512:
                                p.op("P", lambda e, kt=kt, q0=q0: e.tensor_tensor(
                                    eT[:, kt * 512 + q0:kt * 512 + q0 + 128],
                                    eT[:, kt * 512 + q0:kt * 512 + q0 + 128],
                                    self.maskT[:], ALU.mult),
                                    reads=[ek, ("maskT",)], writes=[ek])
                                yield
                            ekeys.append(ek)
                            kt += 1
                        bank, bkey = self.bank()

                        def av(e, hh=hh, c=c, ktmax=ktmax, bank=bank, kv_=kv_):
                            for kt in range(ktmax):
                                q0 = max(0, kt * 128 - c * 512)
                                ins = e.matmul(
                                    bank[0:65, q0:512],
                                    kv_[:, (kt * 2 + hh) * 65:(kt * 2 + hh) * 65 + 65],
                                    eT[:, kt * 512 + q0:(kt + 1) * 512],
                                    start=(kt == 0), stop=(kt == ktmax - 1))
                            return ins
                        p.op("T", av,
                             reads=ekeys + [("kvtok", pr, hh, kt2) for kt2 in range(ktmax)]
                             + [("kvtok_ones", pr)],
                             writes=[bkey])
                        yield
                        p.op("S", lambda e, bank=bank: e.activation(rec[:], bank[64:65, :],
                                                                    AF.Ln),
                             reads=[bkey], writes=[("rec",)], keep=[bkey])
                        yield
                        p.op("S", lambda e: e.activation(recb[:], rec[:], AF.Exp, scale=-1.0),
                             reads=[("rec",)], writes=[("recb",)])
                        yield
                        bank2, bkey2 = self.bank()
                        p.op("T", lambda e, bank2=bank2: e.matmul(
                            bank2[0:64, :], self.ones_bf[0:1, 0:64], recb[:],
                            start=True, stop=True),
                            reads=[("recb",), ("ones_bf",)], writes=[bkey2])
                        yield
                        p.op("V", lambda e, bank2=bank2: e.tensor_copy(bcd[:], bank2[0:64, :]),
                             reads=[bkey2], writes=[("bcd",)])
                        yield
                        dt_, row = divmod(h * 64, 128)
                        p.op("V", lambda e, c=c, dt_=dt_, row=row, bank=bank: e.tensor_tensor(
                            yT[row:row + 64, dt_ * S + c * 512:dt_ * S + (c + 1) * 512],
                            bank[0:64, :], bcd[:], ALU.mult),
                            reads=[bkey, ("bcd",)], writes=[("yT", h, c)])
                        yield

            def run(gen):
                for _ in gen:
                    pass

            def interleave(main, filler, ratio=1):
                while True:
                    done_main = False
                    for _ in range(ratio):
                        try:
                            next(main)
                        except StopIteration:
                            done_main = True
                            break
                    if filler is not None:
                        try:
                            next(filler)
                        except StopIteration:
                            filler = None
                    if done_main:
                        break
                if filler is not None:
                    run(filler)

            NG = H // 2
            run(prep(0))
            for g in range(NG):
                interleave(c3(g), prep(g + 1) if g + 1 < NG else None, ratio=2)

            for t in range(NT):
                okeys, banks = [], []
                for half in range(2):
                    bank, bkey = self.bank()

                    def mm(e, t=t, half=half, bank=bank):
                        for dt in range(DT):
                            ins = e.matmul(
                                bank[:],
                                yT[:, dt * S + t * 128:dt * S + (t + 1) * 128],
                                wo[:, dt * 1024 + half * 512:dt * 1024 + (half + 1) * 512],
                                start=(dt == 0), stop=(dt == DT - 1))
                        return ins
                    p.op("T", mm,
                         reads=[("yT", hh2, t // 4) for hh2 in range(H)] + [("wo",)],
                         writes=[bkey])
                    okeys.append(bkey)
                    banks.append(bank)
                rkey = ("rsum",)
                for half in range(2):
                    p.op("V", lambda e, t=t, half=half, bank=banks[half]: e.tensor_tensor(
                        rsum[:, half * 512:(half + 1) * 512], bank[:],
                        self.r[:, t * 1024 + half * 512:t * 1024 + (half + 1) * 512],
                        ALU.add),
                        reads=[okeys[half], ("r", t)],
                        writes=[rkey])
                self.ln_tile(rkey, rsum[:], sq, s1, s2,
                             self.r[:, t * 1024:(t + 1) * 1024], [("r", t)], 1024, EPS_LN)
            p.barrier()


    def l0_ffn(self, router_w, w1, w2):
        nc, p = self.nc, self.p
        S, NT = self.S, self.NT
        with contextlib.ExitStack() as st:
            sb = lambda name, shape, dt: self._sb(st, name, shape, dt)
            rT = sb("rT", [128, DT * S], BF16)
            self.transpose_r(rT)
            rw = sb("rw", [8, 1024], BF16)
            rwT = sb("rwT", [128, DT * 8], BF16)
            cw = sb("cw", [128, NT * 8], F32)
            pr = sb("pr", [128, 8], F32)
            t8 = sb("t8", [128, 8], F32)
            g8 = sb("g8", [128, 8], F32)
            c1 = sb("c1", [128, 6], F32)
            we1 = [sb("we1_%d" % i, [128, DT * 1024], BF16) for i in range(2)]
            we2 = [sb("we2_%d" % i, [128, DT * 1024], BF16) for i in range(2)]
            aT2 = [sb("aT", [128, DT * (S // 2)], BF16) for _ in range(2)]
            rlu = sb("rlu", [128, 512], F32)
            rsum = sb("rsum", [128, (NT // 2) * 1024], F32)
            sq = sb("sq", [128, 1024], F32)
            s1 = sb("s1", [128, 4], F32)
            s2 = sb("s2", [128, 4], F32)

            p.op("P", lambda e: e.dma_start(rw[:], router_w), writes=[("rw",)], dma=True)
            for dt in range(DT):
                tb, tkey = self.tbank()
                p.op("T", lambda e, dt=dt, tb=tb: e.transpose(
                    tb[0:128, 0:8], rw[0:8, dt * 128:(dt + 1) * 128], self.ident[0:8, 0:8]),
                    reads=[("rw",), ("ident",)], writes=[tkey])
                p.op("V", lambda e, dt=dt, tb=tb: e.tensor_copy(
                    rwT[:, dt * 8:(dt + 1) * 8], tb[0:128, 0:8]),
                    reads=[tkey], writes=[("rwT",)])
            for t in range(NT):
                bank, bkey = self.bank()

                def mm(e, t=t, bank=bank):
                    for dt in range(DT):
                        ins = e.matmul(bank[:, 0:8],
                                       rT[:, dt * S + t * 128:dt * S + (t + 1) * 128],
                                       rwT[:, dt * 8:(dt + 1) * 8],
                                       start=(dt == 0), stop=(dt == DT - 1))
                    return ins
                p.op("T", mm, reads=[("rwT",)] + [("rT", dt, t) for dt in range(DT)],
                     writes=[bkey])
                p.op("S", lambda e, bank=bank: e.activation(pr[:], bank[:, 0:8], AF.Exp,
                                                            scale=-1.0),
                     reads=[bkey], writes=[("pr",)])
                p.op("V", lambda e: e.tensor_scalar_add(pr[:], pr[:], 1.0),
                     reads=[("pr",)], writes=[("pr",)])
                p.op("V", lambda e: e.reciprocal(pr[:], pr[:]),
                     reads=[("pr",)], writes=[("pr",)])
                p.op("V", lambda e: e.reduce_max(c1[:, 0:1], pr[:], axis=AXX),
                     reads=[("pr",)], writes=[("c1",)])
                p.op("V", lambda e: e.tensor_scalar(t8[:], pr[:], c1[:, 0:1], None,
                                                    ALU.subtract),
                     reads=[("pr",), ("c1",)], writes=[("t8",)])
                p.op("V", lambda e: e.tensor_single_scalar(g8[:], t8[:], 0.0, ALU.is_equal),
                     reads=[("t8",)], writes=[("g8",)])
                p.op("V", lambda e: e.scalar_tensor_tensor(t8[:], g8[:], -1e30, t8[:],
                                                           ALU.mult, ALU.add),
                     reads=[("t8",), ("g8",)], writes=[("t8",)])
                p.op("V", lambda e: e.reduce_max(c1[:, 1:2], t8[:], axis=AXX),
                     reads=[("t8",)], writes=[("c1b",)])
                p.op("V", lambda e: e.tensor_tensor(c1[:, 2:3], c1[:, 1:2], c1[:, 0:1],
                                                    ALU.add),
                     reads=[("c1",), ("c1b",)], writes=[("c1c",)])
                p.op("V", lambda e: e.tensor_scalar(c1[:, 3:4], c1[:, 2:3], c1[:, 0:1],
                                                    1e-20, ALU.add, ALU.add),
                     reads=[("c1",), ("c1c",)], writes=[("c1d",)])
                p.op("V", lambda e: e.reciprocal(c1[:, 4:5], c1[:, 3:4]),
                     reads=[("c1d",)], writes=[("c1e",)])
                p.op("V", lambda e: e.tensor_scalar(g8[:], pr[:], c1[:, 2:3], None,
                                                    ALU.is_ge),
                     reads=[("pr",), ("c1c",)], writes=[("g8",)])
                p.op("V", lambda e: e.tensor_tensor(t8[:], pr[:], g8[:], ALU.mult),
                     reads=[("pr",), ("g8",)], writes=[("t8",)])
                p.op("V", lambda e, t=t: e.tensor_scalar(cw[:, t * 8:(t + 1) * 8], t8[:],
                                                         c1[:, 4:5], None, ALU.mult),
                     reads=[("t8",), ("c1e",)], writes=[("cw", t)])

            HS = S // 2
            HT = NT // 2
            for half in range(2):
                for t in range(HT):
                    tg = half * HT + t
                    p.op("V", lambda e, t=t, tg=tg: e.tensor_copy(
                        rsum[:, t * 1024:(t + 1) * 1024],
                        self.r[:, tg * 1024:(tg + 1) * 1024]),
                        reads=[("r", tg)], writes=[("rsum", t)])
                for ex in range(8):
                    wb1, wb2 = we1[ex % 2], we2[ex % 2]
                    k1, k2 = ("we1", ex % 2), ("we2", ex % 2)
                    aT = aT2[ex % 2]
                    aki = ex % 2
                    for dh in range(2):
                        p.op("P", lambda e, ex=ex, wb1=wb1, dh=dh: e.dma_start(
                            wb1[:, dh * 4096:(dh + 1) * 4096].rearrange(
                                "p (dt c) -> p dt c", c=1024),
                            w1[dh * 512:(dh + 1) * 512,
                               ex * 1024:(ex + 1) * 1024].rearrange(
                                "(dt p) c -> p dt c", p=128)),
                            writes=[k1], dma=True)
                        p.op("P", lambda e, ex=ex, wb2=wb2, dh=dh: e.dma_start(
                            wb2[:, dh * 4096:(dh + 1) * 4096].rearrange(
                                "p (dt c) -> p dt c", c=1024),
                            w2[ex * 1024 + dh * 512:ex * 1024 + (dh + 1) * 512, :].rearrange(
                                "(dt p) c -> p dt c", p=128)),
                            writes=[k2], dma=True)
                    for wf in range(DT):
                        for c in range(HS // 512):
                            bank, bkey = self.bank()
                            cg = half * (HS // 512) + c

                            def mm(e, wf=wf, cg=cg, bank=bank, wb1=wb1):
                                for dt in range(DT):
                                    ins = e.matmul(
                                        bank[:],
                                        wb1[:, dt * 1024 + wf * 128:dt * 1024 + wf * 128 + 128],
                                        rT[:, dt * S + cg * 512:dt * S + (cg + 1) * 512],
                                        start=(dt == 0), stop=(dt == DT - 1))
                                return ins
                            p.op("T", mm, reads=[k1] + [("rT", dt, tt) for dt in range(DT)
                                                        for tt in range(cg * 4, cg * 4 + 4)],
                                 writes=[bkey])
                            p.op("S", lambda e, bank=bank: e.activation(
                                rlu[:], bank[:], AF.Relu),
                                reads=[bkey], writes=[("rlu",)])
                            p.op("S", lambda e, wf=wf, c=c, aT=aT: e.activation(
                                aT[:, wf * HS + c * 512:wf * HS + (c + 1) * 512],
                                rlu[:], AF.Square),
                                reads=[("rlu",)], writes=[("aT", aki, wf, c)])
                    for t in range(HT):
                        for ch in range(2):
                            bank, bkey = self.bank()

                            def mm(e, t=t, ch=ch, bank=bank, wb2=wb2, aT=aT):
                                for wf in range(DT):
                                    ins = e.matmul(
                                        bank[:],
                                        aT[:, wf * HS + t * 128:wf * HS + (t + 1) * 128],
                                        wb2[:, wf * 1024 + ch * 512:wf * 1024 + (ch + 1) * 512],
                                        start=(wf == 0), stop=(wf == DT - 1))
                                return ins
                            p.op("T", mm, reads=[k2] + [("aT", aki, wf, t // 4)
                                                        for wf in range(DT)],
                                 writes=[bkey])
                            tg = half * HT + t
                            p.op("V", lambda e, t=t, ch=ch, tg=tg, ex=ex, bank=bank:
                                 e.scalar_tensor_tensor(
                                     rsum[:, t * 1024 + ch * 512:t * 1024 + (ch + 1) * 512],
                                     bank[:], cw[:, tg * 8 + ex:tg * 8 + ex + 1],
                                     rsum[:, t * 1024 + ch * 512:t * 1024 + (ch + 1) * 512],
                                     ALU.mult, ALU.add),
                                 reads=[bkey, ("cw", tg), ("rsum", t)], writes=[("rsum", t)])
                for t in range(HT):
                    tg = half * HT + t
                    self.ln_tile(("rsum", t), rsum[:, t * 1024:(t + 1) * 1024], sq, s1, s2,
                                 self.r[:, tg * 1024:(tg + 1) * 1024], [("r", tg)],
                                 1024, EPS_LN)
            p.barrier()


    def l1_attn(self, Wq, Wk, Wv):
        nc, p = self.nc, self.p
        S, NT = self.S, self.NT
        with contextlib.ExitStack() as st:
            sb = lambda name, shape, dt: self._sb(st, name, shape, dt)
            rT = sb("rT", [128, DT * S], BF16)
            self.transpose_r(rT)
            rsum = sb("rsumf", [128, NT * 1024], BF16)
            wqT = sb("wqT", [128, DT * 1024], BF16)
            wkvT = sb("wkvT", [128, DT * 128], BF16)
            qTa2 = [sb("qTa", [65, 2 * S], BF16) for _ in range(2)]
            kTa = sb("kTa", [65, S], BF16)
            va = sb("va", [128, NT * 65], BF16)
            mneg2 = [sb("mneg", [128, NT], F32) for _ in range(2)]
            mnegb2 = [sb("mnegb", [128, NT], BF16) for _ in range(2)]
            mrow2 = [sb("mrow", [NT, 128], BF16) for _ in range(2)]
            yn = sb("yn", [64, 512], BF16)
            rec = sb("rec", [1, 512], F32)
            recb = sb("recb", [1, 512], BF16)
            bcd = sb("bcd", [64, 512], F32)
            sq = sb("sq", [128, 1024], F32)
            s1 = sb("s1", [128, 4], F32)
            s2 = sb("s2", [128, 4], F32)

            wst = st.enter_context(contextlib.ExitStack())
            wraw = self._sb(wst, "wraw", [128, DT * 1024], BF16)
            p.op("P", lambda e: e.dma_start(
                wraw[:].rearrange("p (ft c) -> p ft c", c=1024),
                Wq.rearrange("(ft p) c -> p ft c", p=128)),
                writes=[("wraw",)], dma=True)
            for ft in range(DT):
                for dt in range(DT):
                    i = ft * DT + dt
                    tb, tkey = self.tbank()
                    p.op("T", lambda e, ft=ft, dt=dt, tb=tb: e.transpose(
                        tb[:, 0:128], wraw[:, ft * 1024 + dt * 128:ft * 1024 + (dt + 1) * 128],
                        self.ident[:]),
                        reads=[("wraw",), ("ident",)], writes=[tkey])
                    p.op("V" if i % 2 == 0 else "S",
                         lambda e, ft=ft, dt=dt, tb=tb, m=(i % 2): (
                             e.tensor_copy if m == 0 else e.copy)(
                             wqT[:, dt * 1024 + ft * 128:dt * 1024 + (ft + 1) * 128],
                             tb[:, 0:128]),
                         reads=[tkey], writes=[("wqT", dt)])
            p.op("P", lambda e: e.dma_start(
                wraw[0:64, 0:1024].rearrange("p (dt c) -> p dt c", c=128),
                Wk.rearrange("p (dt c) -> p dt c", c=128)),
                writes=[("wraw",), ("wrawk",)], dma=True)
            p.op("P", lambda e: e.dma_start(
                wraw[64:128, 0:1024].rearrange("p (dt c) -> p dt c", c=128),
                Wv.rearrange("p (dt c) -> p dt c", c=128)),
                writes=[("wraw",), ("wrawv",)], dma=True)
            for dt in range(DT):
                tb, tkey = self.tbank()
                p.op("T", lambda e, dt=dt, tb=tb: e.transpose(
                    tb[:, 0:128], wraw[:, dt * 128:(dt + 1) * 128], self.ident[:]),
                    reads=[("wrawk",), ("wrawv",), ("ident",)], writes=[tkey])
                p.op("V", lambda e, dt=dt, tb=tb: e.tensor_copy(
                    wkvT[:, dt * 128:(dt + 1) * 128], tb[:, 0:128]),
                    reads=[tkey], writes=[("wkvT", dt)])
            p.barrier()
            wst.close()
            eT2 = [sb("eT", [128, NT * 512], BF16) for _ in range(2)]

            p.op("V", lambda e: e.memset(kTa[64:65, :], 1.0), writes=[("kTa_ones",)])
            p.op("V", lambda e: e.memset(
                va[:].rearrange("p (t c) -> p t c", c=65)[:, :, 64:65], 1.0),
                writes=[("va_ones",)])
            for c in range(S // 512):
                bank, bkey = self.bank()

                def mm(e, c=c, bank=bank):
                    for dt in range(DT):
                        ins = e.matmul(bank[0:64, :],
                                       wkvT[:, dt * 128:dt * 128 + 64],
                                       rT[:, dt * S + c * 512:dt * S + (c + 1) * 512],
                                       start=(dt == 0), stop=(dt == DT - 1))
                    return ins
                p.op("T", mm, reads=[("wkvT", dt2) for dt2 in range(DT)]
                     + [("rT", dt2, tt) for dt2 in range(DT) for tt in range(c * 4, c * 4 + 4)],
                     writes=[bkey])
                p.op("V", lambda e, c=c, bank=bank: e.tensor_copy(
                    kTa[0:64, c * 512:(c + 1) * 512], bank[0:64, :]),
                    reads=[bkey], writes=[("kTa", c)])
            for t in range(NT):
                bank, bkey = self.bank()

                def mm(e, t=t, bank=bank):
                    for dt in range(DT):
                        ins = e.matmul(bank[:, 0:64],
                                       rT[:, dt * S + t * 128:dt * S + (t + 1) * 128],
                                       wkvT[:, dt * 128 + 64:dt * 128 + 128],
                                       start=(dt == 0), stop=(dt == DT - 1))
                    return ins
                p.op("T", mm, reads=[("wkvT", dt2) for dt2 in range(DT)]
                     + [("rT", dt2, t) for dt2 in range(DT)],
                     writes=[bkey])
                p.op("V", lambda e, t=t, bank=bank: e.tensor_copy(
                    va[:, t * 65:t * 65 + 64], bank[:, 0:64]),
                    reads=[bkey], writes=[("va", t)])

            def prep(g):
                pr = g % 2
                qTa, mneg, mnegb, mrow = qTa2[pr], mneg2[pr], mnegb2[pr], mrow2[pr]
                for c in range(S // 512):
                    bank, bkey = self.bank()

                    def mm(e, g=g, c=c, bank=bank):
                        for dt in range(DT):
                            ins = e.matmul(
                                bank[:],
                                wqT[:, dt * 1024 + g * 128:dt * 1024 + (g + 1) * 128],
                                rT[:, dt * S + c * 512:dt * S + (c + 1) * 512],
                                start=(dt == 0), stop=(dt == DT - 1))
                        return ins
                    p.op("T", mm, reads=[("wqT", dt2) for dt2 in range(DT)]
                         + [("rT", dt2, tt) for dt2 in range(DT)
                            for tt in range(c * 4, c * 4 + 4)],
                         writes=[bkey])
                    yield
                    for hh in range(2):
                        p.op("V", lambda e, hh=hh, c=c, bank=bank, qTa=qTa: e.tensor_scalar_mul(
                            qTa[0:64, hh * S + c * 512:hh * S + (c + 1) * 512],
                            bank[hh * 64:hh * 64 + 64, :], 0.125),
                            reads=[bkey], writes=[("qTa", pr, hh, c)])
                        yield
                for hh in range(2):
                    for qt in range(NT):
                        c = qt // 4
                        bank, bkey = self.bank()
                        p.op("T", lambda e, hh=hh, qt=qt, c=c, bank=bank, qTa=qTa: e.matmul(
                            bank[:],
                            qTa[0:64, hh * S + qt * 128:hh * S + (qt + 1) * 128],
                            kTa[0:64, c * 512:(c + 1) * 512],
                            start=True, stop=True),
                            reads=[("qTa", pr, hh, qt // 4), ("kTa", c)], writes=[bkey])
                        yield
                        p.op("V", lambda e, qt=qt, bank=bank, mneg=mneg: e.reduce_max(
                            mneg[:, qt:qt + 1], bank[:], axis=AXX, negate=True),
                            reads=[bkey], writes=[("mneg", pr, qt)])
                        yield
                    p.op("V", lambda e, mneg=mneg, mnegb=mnegb: e.tensor_copy(mnegb[:], mneg[:]),
                         reads=[("mneg", pr, qt2) for qt2 in range(NT)],
                         writes=[("mnegb", pr)])
                    yield
                    mtb, mtkey = self.tbank()
                    p.op("T", lambda e, mnegb=mnegb, mtb=mtb: e.transpose(
                        mtb[0:NT, 0:128], mnegb[:], self.ident[:]),
                        reads=[("mnegb", pr), ("ident",)], writes=[mtkey])
                    yield
                    p.op("V", lambda e, mrow=mrow, mtb=mtb: e.tensor_copy(
                        mrow[:], mtb[0:NT, 0:128]),
                        reads=[mtkey], writes=[("mrow", pr)])
                    yield
                    p.op("P", lambda e, hh=hh, qTa=qTa, mrow=mrow: e.dma_start(
                        qTa[64:65, hh * S:(hh + 1) * S].rearrange("o (t c) -> o t c", c=128),
                        mrow[:, :]),
                        reads=[("mrow", pr)], writes=[("qTa64", pr, hh)], dma=True)
                    yield

            def c3(g):
                pr = g % 2
                qTa = qTa2[pr]
                for hh in range(2):
                    h = 2 * g + hh
                    for c in range(S // 512):
                        eT = eT2[c % 2]
                        eki = c % 2
                        ekeys = []
                        for kt in range(NT):
                            bank, bkey = self.bank()
                            p.op("T", lambda e, hh=hh, c=c, kt=kt, bank=bank, qTa=qTa:
                                 e.matmul(
                                     bank[:, :],
                                     kTa[:, kt * 128:(kt + 1) * 128],
                                     qTa[:, hh * S + c * 512:hh * S + (c + 1) * 512],
                                     start=True, stop=True),
                                 reads=[("kTa", kt // 4), ("kTa_ones",),
                                        ("qTa", pr, hh, c), ("qTa64", pr, hh)],
                                 writes=[bkey])
                            yield
                            ek = ("eT", eki, kt)
                            p.op("S", lambda e, kt=kt, bank=bank, eT=eT: e.activation(
                                eT[:, kt * 512:(kt + 1) * 512], bank[:, :], AF.Exp),
                                reads=[bkey], writes=[ek])
                            yield
                            ekeys.append(ek)
                        bank, bkey = self.bank()

                        def av(e, bank=bank, eT=eT):
                            for kt in range(NT):
                                ins = e.matmul(
                                    bank[0:65, :],
                                    va[:, kt * 65:(kt + 1) * 65],
                                    eT[:, kt * 512:(kt + 1) * 512],
                                    start=(kt == 0), stop=(kt == NT - 1))
                            return ins
                        p.op("T", av, reads=ekeys
                             + [("va", kt2) for kt2 in range(NT)] + [("va_ones",)],
                             writes=[bkey])
                        yield
                        p.op("S", lambda e, bank=bank: e.activation(rec[:], bank[64:65, :],
                                                                    AF.Ln),
                             reads=[bkey], writes=[("rec",)], keep=[bkey])
                        yield
                        p.op("S", lambda e: e.activation(recb[:], rec[:], AF.Exp, scale=-1.0),
                             reads=[("rec",)], writes=[("recb",)])
                        yield
                        bank2, bkey2 = self.bank()
                        p.op("T", lambda e, bank2=bank2: e.matmul(
                            bank2[0:64, :], self.ones_bf[0:1, 0:64], recb[:],
                            start=True, stop=True),
                            reads=[("recb",), ("ones_bf",)], writes=[bkey2])
                        yield
                        p.op("V", lambda e, bank2=bank2: e.tensor_copy(bcd[:], bank2[0:64, :]),
                             reads=[bkey2], writes=[("bcd",)])
                        yield
                        p.op("V", lambda e, bank=bank: e.tensor_tensor(
                            yn[:], bank[0:64, :], bcd[:], ALU.mult),
                            reads=[bkey, ("bcd",)], writes=[("yn",)])
                        yield
                        for i4 in range(4):
                            tb, tkey = self.tbank()
                            p.op("T", lambda e, i4=i4, tb=tb: e.transpose(
                                tb[:, 0:64], yn[:, i4 * 128:(i4 + 1) * 128],
                                self.ident[0:64, 0:64]),
                                reads=[("yn",), ("ident",)], writes=[tkey])
                            yield
                            tg = c * 4 + i4
                            p.op("V", lambda e, tg=tg, h=h, tb=tb: e.tensor_tensor(
                                rsum[:, tg * 1024 + h * 64:tg * 1024 + h * 64 + 64],
                                tb[:, 0:64],
                                self.r[:, tg * 1024 + h * 64:tg * 1024 + h * 64 + 64],
                                ALU.add),
                                reads=[tkey, ("r", tg)], writes=[("rs", tg)])
                            yield

            def run(gen):
                for _ in gen:
                    pass

            def interleave(main, filler, ratio=1):
                while True:
                    done_main = False
                    for _ in range(ratio):
                        try:
                            next(main)
                        except StopIteration:
                            done_main = True
                            break
                    if filler is not None:
                        try:
                            next(filler)
                        except StopIteration:
                            filler = None
                    if done_main:
                        break
                if filler is not None:
                    run(filler)

            NG = H // 2
            run(prep(0))
            for g in range(NG):
                interleave(c3(g), prep(g + 1) if g + 1 < NG else None, ratio=2)

            for t in range(NT):
                self.ln_tile(("rs", t), rsum[:, t * 1024:(t + 1) * 1024], sq, s1, s2,
                             self.r[:, t * 1024:(t + 1) * 1024], [("r", t)], 1024, EPS_LN)
            p.barrier()


    def wquant_pass1(self, w_dram, rows, cols, tag):
        p = self.p
        acc, onef = self.wacc, self.onef
        RCH, CCH = 128, 512
        nch, ncc = rows // RCH, cols // CCH
        ac = 0 if tag == "a" else 1
        pc = 2 if tag == "a" else 3
        p.op("V", lambda e: e.memset(onef[:], 1.0), writes=[("onef",)])
        first = True
        for i in range(nch):
            for j in range(ncc):
                ii = i * ncc + j
                wc, wk = self.wch[ii % 2], ("wchp", ii % 2)
                p.op("Q", lambda e, i=i, j=j, wc=wc: e.dma_start(
                    wc[:], w_dram[i * RCH:(i + 1) * RCH, j * CCH:(j + 1) * CCH]),
                    writes=[wk], dma=True)
                if first:
                    p.op("V", lambda e, wc=wc, ac=ac: e.reduce_sum(
                        acc[:, ac:ac + 1], wc[:], axis=AXX, apply_absolute_value=True),
                        reads=[wk], writes=[("wacc", ac)])
                    first = False
                else:
                    p.op("V", lambda e, wc=wc, pc=pc: e.reduce_sum(
                        acc[:, pc:pc + 1], wc[:], axis=AXX, apply_absolute_value=True),
                        reads=[wk], writes=[("wacc2", pc)])
                    p.op("V", lambda e, ac=ac, pc=pc: e.tensor_tensor(
                        acc[:, ac:ac + 1], acc[:, ac:ac + 1], acc[:, pc:pc + 1], ALU.add),
                        reads=[("wacc", ac), ("wacc2", pc)], writes=[("wacc", ac)])

    def wquant_pass1_fin(self, rows, cols, tag):
        p = self.p
        acc, onef = self.wacc, self.onef
        ac = 0 if tag == "a" else 1
        mccol = 2 if tag == "a" else 3
        bank, bkey = self.bank()
        p.op("T", lambda e, bank=bank, ac=ac: e.matmul(
            bank[0:1, 0:1], acc[:, ac:ac + 1], onef[:], start=True, stop=True),
             reads=[("wacc", ac), ("onef",)], writes=[bkey])
        p.op("V", lambda e, bank=bank, mccol=mccol: e.tensor_scalar(
            acc[0:1, mccol:mccol + 1], bank[0:1, 0:1], 1.0 / (rows * cols), 1e-5,
            ALU.mult, ALU.max),
            reads=[bkey], writes=[("wmc", tag), ("wacc2", mccol)])
        bank2, bkey2 = self.bank()
        p.op("T", lambda e, bank2=bank2, mccol=mccol: e.matmul(
            bank2[:, 0:1], self.onerow[:], acc[0:1, mccol:mccol + 1], start=True, stop=True),
            reads=[("wmc", tag), ("onerow",)], writes=[bkey2])
        p.op("V", lambda e, bank2=bank2, tag=tag: e.tensor_copy(self.swi[tag][:],
                                                                bank2[:, 0:1]),
             reads=[bkey2], writes=[(tag, "swinv")])
        p.op("V", lambda e, tag=tag: e.reciprocal(self.swv[tag][:], self.swi[tag][:]),
             reads=[(tag, "swinv")], writes=[(tag, "sw")])

    def wquant_pass2(self, w_dram, rows, cols, wqT, tag, store):
        nc, p = self.nc, self.p
        sw = self.swv[tag]
        RCH, CCH = 128, 512
        nch, ncc = rows // RCH, cols // CCH
        for i in range(nch):
            for j in range(ncc):
                ii = i * ncc + j
                wc, wk = self.wch[ii % 2], ("wchp", ii % 2)
                q = "Q" if ii % 2 == 0 else "P"
                p.op(q, lambda e, i=i, j=j, wc=wc: e.dma_start(
                    wc[:], w_dram[i * RCH:(i + 1) * RCH, j * CCH:(j + 1) * CCH]),
                    writes=[wk], dma=True)
                p.op("V", lambda e, wc=wc: e.tensor_scalar(
                    wc[:], wc[:], sw[:], None, ALU.mult),
                    reads=[wk, (tag, "sw")], writes=[wk])
                p.op("V", lambda e, wc=wc: e.tensor_scalar(
                    wc[:], wc[:], self.magic[:], self.magic[:], ALU.add, ALU.subtract),
                    reads=[wk, ("magic",)], writes=[wk])
                p.op("V", lambda e, wc=wc: e.tensor_scalar(
                    wc[:], wc[:], 1.0, -1.0, ALU.min, ALU.max),
                    reads=[wk], writes=[wk])
                for cl in range(CCH // 128):
                    ct = j * (CCH // 128) + cl
                    bank, bkey = self.bank()
                    p.op("T", lambda e, cl=cl, wc=wc, bank=bank: e.transpose(
                        bank[:, 0:128], wc[:, cl * 128:(cl + 1) * 128], self.identf[:]),
                        reads=[wk, ("identf",)], writes=[bkey])
                    dst = wqT[:, ct * rows + i * RCH:ct * rows + (i + 1) * RCH]
                    if ct % 2 == 0:
                        p.op("V", lambda e, dst=dst, bank=bank: e.tensor_copy(
                            dst, bank[:, 0:128]),
                            reads=[bkey], writes=[(tag, "wqT")])
                    else:
                        p.op("S", lambda e, dst=dst, bank=bank: e.copy(
                            dst, bank[:, 0:128]),
                            reads=[bkey], writes=[(tag, "wqT")])
        if store is not None:
            p.op("P", lambda e: e.dma_start(store.ap(), wqT[:]),
                 reads=[(tag, "wqT")], writes=[(tag, "wq_dram")], dma=True)

    def wquant_load(self, wqT, tag, store):
        p = self.p
        n = 8
        step = wqT.shape[1] // n
        for i in range(n):
            q = "Q" if i % 2 == 0 else "P"
            p.op(q, lambda e, i=i: e.dma_start(
                wqT[:, i * step:(i + 1) * step], store.ap()[:, i * step:(i + 1) * step]),
                reads=[(tag, "wq_dram")], writes=[(tag, "wqT")], dma=True)

    def l1_ffn(self, norm1_w, w1, act_scale, act_bias, norm2_w, w2, out_dram=None, out_s=0,
               first_seq=True):
        nc, p = self.nc, self.p
        S, NT = self.S, self.NT
        FH = FFN_H
        HT = FH // 128
        if "a" not in self.wq_dram:
            self.wq_dram["a"] = nc.dram_tensor("wqs_a", [128, DT * FH], BF16)
            self.wq_dram["b"] = nc.dram_tensor("wqs_b", [128, HT * 1024], BF16)
        with contextlib.ExitStack() as st:
            sb = lambda name, shape, dt: self._sb(st, name, shape, dt)
            w1qT = sb("w1qT", [128, DT * FH], BF16)
            w2qT = sb("w2qT", [128, HT * 1024], BF16)
            for c in range(2):
                p.op("P", lambda e, c=c: e.dma_start(
                    nrow[0:1, :], norm1_w[None, c * 512:(c + 1) * 512]),
                    writes=[("nrow",)], dma=True)
                bank, bkey = self.bank()
                p.op("T", lambda e, bank=bank: e.matmul(
                    bank[:], self.ones_bf[0:1, 0:128], nrow[0:1, :],
                    start=True, stop=True),
                    reads=[("nrow",), ("ones_bf",)], writes=[bkey])
                p.op("V", lambda e, c=c, bank=bank: e.tensor_copy(
                    n1b[:, c * 512:(c + 1) * 512], bank[:]),
                    reads=[bkey], writes=[("n1b",)])
            for c in range(FH // 512):
                p.op("P", lambda e, c=c: e.dma_start(
                    nrow[0:1, :], norm2_w[None, c * 512:(c + 1) * 512]),
                    writes=[("nrow",)], dma=True)
                bank, bkey = self.bank()
                p.op("T", lambda e, bank=bank: e.matmul(
                    bank[:], self.ones_bf[0:1, 0:128], nrow[0:1, :],
                    start=True, stop=True),
                    reads=[("nrow",), ("ones_bf",)], writes=[bkey])
                p.op("V", lambda e, c=c, bank=bank: e.tensor_copy(
                    n2b[:, c * 512:(c + 1) * 512], bank[:]),
                    reads=[bkey], writes=[("n2b",)])
            p.op("P", lambda e: e.dma_start(scal[0:1, 0:1], act_scale[None, :]),
                 writes=[("sc0",)], dma=True)
            p.op("P", lambda e: e.dma_start(scal[0:1, 1:2], act_bias[None, :]),
                 writes=[("sc1",)], dma=True)
            bank, bkey = self.bank()
            p.op("T", lambda e, bank=bank: e.matmul(
                bank[:, 0:2], self.onerow[:], scal[0:1, 0:2], start=True, stop=True),
                reads=[("sc0",), ("sc1",), ("onerow",)], writes=[bkey])
            p.op("V", lambda e, bank=bank: e.tensor_copy(asb[:], bank[:, 0:2]),
                 reads=[bkey], writes=[("asb",)])

            sw1i, sw2i = self.swi["a"], self.swi["b"]
            if first_seq:
                self.wquant_pass1_fin(FH, 1024, "a")
                self.wquant_pass1_fin(1024, FH, "b")
                self.wquant_pass2(w1, FH, 1024, w1qT, "a", self.wq_dram["a"])
                self.wquant_pass2(w2, 1024, FH, w2qT, "b", self.wq_dram["b"])
            else:
                self.wquant_load(w1qT, "a", self.wq_dram["a"])
                self.wquant_load(w2qT, "b", self.wq_dram["b"])
            n1b = sb("n1b", [128, 1024], BF16)
            n2b = sb("n2b", [128, FH], BF16)
            asb = sb("asb", [128, 2], F32)
            nrow = sb("nrow", [1, 512], BF16)
            scal = sb("scal", [128, 8], F32)
            xn = sb("xn", [128, 1024], F32)
            xq = sb("xq", [128, 1024], BF16)
            xqT = sb("xqT", [128, DT * 128], BF16)
            h2 = sb("h2", [128, FH], BF16)
            xq2 = h2
            xq2T = sb("xq2T", [128, HT * 128], BF16)
            tc1 = xn[:, 0:512]
            tc2 = xn[:, 512:1024]
            rsum = sb("rsum", [128, 1024], F32)
            s1 = sb("s1", [128, 4], F32)
            s2 = sb("s2", [128, 4], F32)
            rs = sb("rs", [128, 8], F32)

            for t in range(NT):
                rk = ("r", t)
                rt = self.r[:, t * 1024:(t + 1) * 1024]
                p.op("S", lambda e, rt=rt: e.activation(xn[:], rt, AF.Square),
                     reads=[rk], writes=[("xn",)])
                p.op("V", lambda e: e.reduce_sum(rs[:, 0:1], xn[:], axis=AXX),
                     reads=[("xn",)], writes=[("s1a",)])
                p.op("V", lambda e: e.tensor_scalar(rs[:, 1:2], rs[:, 0:1], 1.0 / 1024,
                                                    EPS_NORM, ALU.mult, ALU.add),
                     reads=[("s1a",)], writes=[("s1b",)])
                p.op("S", lambda e: e.activation(rs[:, 2:3], rs[:, 1:2], AF.Sqrt),
                     reads=[("s1b",)], writes=[("s1c",)])
                p.op("V", lambda e: e.reciprocal(rs[:, 3:4], rs[:, 2:3]),
                     reads=[("s1c",)], writes=[("s1d",)])
                p.op("V", lambda e, rt=rt: e.tensor_scalar(xn[:], rt, rs[:, 3:4], None,
                                                           ALU.mult),
                     reads=[rk, ("s1d",)], writes=[("xn",)])
                p.op("V", lambda e: e.tensor_tensor(xn[:], xn[:], n1b[:], ALU.mult),
                     reads=[("xn",), ("n1b",)], writes=[("xn",)])
                p.op("V", lambda e: e.reduce_max(scal[:, 4:5], xn[:], axis=AXX,
                                                 apply_absolute_value=True),
                     reads=[("xn",)], writes=[("am1",)])
                p.op("V", lambda e: e.tensor_scalar(scal[:, 5:6], scal[:, 4:5], 1e-5,
                                                    1.0 / 127.0, ALU.max, ALU.mult),
                     reads=[("am1",)], writes=[("sxi",)])
                p.op("V", lambda e: e.reciprocal(scal[:, 6:7], scal[:, 5:6]),
                     reads=[("sxi",)], writes=[("sx",)])
                p.op("V", lambda e: e.tensor_scalar(xn[:], xn[:], scal[:, 6:7], None,
                                                    ALU.mult),
                     reads=[("xn",), ("sx",)], writes=[("xn",)])
                p.op("V", lambda e: e.tensor_scalar(xn[:], xn[:], self.magic[:],
                                                    self.magic[:], ALU.add, ALU.subtract),
                     reads=[("xn",), ("magic",)], writes=[("xn",)])
                p.op("V", lambda e: e.tensor_scalar(xq[:], xn[:], 127.0, -128.0,
                                                    ALU.min, ALU.max),
                     reads=[("xn",)], writes=[("xq",)])
                for dt in range(DT):
                    tb, tkey = self.tbank()
                    p.op("T", lambda e, dt=dt, tb=tb: e.transpose(
                        tb[:, 0:128], xq[:, dt * 128:(dt + 1) * 128], self.ident[:]),
                        reads=[("xq",), ("ident",)], writes=[tkey])
                    p.op("V", lambda e, dt=dt, tb=tb: e.tensor_copy(
                        xqT[:, dt * 128:(dt + 1) * 128], tb[:, 0:128]),
                        reads=[tkey], writes=[("xqT", dt)])
                p.op("V", lambda e: e.tensor_tensor(scal[:, 7:8], scal[:, 5:6], sw1i[:],
                                                    ALU.mult),
                     reads=[("sxi",), ("a", "swinv")], writes=[("hsc",)])
                p.op("V", lambda e: e.tensor_tensor(scal[:, 7:8], scal[:, 7:8], scal[:, 7:8],
                                                    ALU.mult),
                     reads=[("hsc",)], writes=[("hsc",)])
                p.op("V", lambda e: e.tensor_scalar(scal[:, 7:8], scal[:, 7:8], asb[:, 0:1],
                                                    None, ALU.mult),
                     reads=[("hsc",), ("asb",)], writes=[("hsc",)])
                for ch in range(FH // 512):
                    bank, bkey = self.bank()

                    def mm(e, ch=ch, bank=bank):
                        for dt in range(DT):
                            ins = e.matmul(
                                bank[:],
                                xqT[:, dt * 128:(dt + 1) * 128],
                                w1qT[:, dt * FH + ch * 512:dt * FH + (ch + 1) * 512],
                                start=(dt == 0), stop=(dt == DT - 1))
                        return ins
                    p.op("T", mm, reads=[("xqT", dt2) for dt2 in range(DT)]
                         + [("a", "wqT")],
                         writes=[bkey])
                    p.op("S", lambda e, bank=bank: e.activation(tc1, bank[:], AF.Relu),
                         reads=[bkey], writes=[("xn",)])
                    p.op("V", lambda e: e.tensor_tensor(tc2, tc1, tc1, ALU.mult),
                         reads=[("xn",)], writes=[("xn",)])
                    p.op("V", lambda e, ch=ch: e.tensor_scalar(
                        h2[:, ch * 512:(ch + 1) * 512], tc2, scal[:, 7:8], asb[:, 1:2],
                        ALU.mult, ALU.add),
                        reads=[("xn",), ("hsc",), ("asb",)], writes=[("h2", ch)])
                for ch in range(FH // 512):
                    p.op("S", lambda e, ch=ch: e.activation(
                        tc1, h2[:, ch * 512:(ch + 1) * 512], AF.Square),
                        reads=[("h2", ch)], writes=[("xn",)])
                    if ch == 0:
                        p.op("V", lambda e: e.reduce_sum(rs[:, 4:5], tc1, axis=AXX),
                             reads=[("xn",)], writes=[("s2a",)])
                    else:
                        p.op("V", lambda e: e.reduce_sum(rs[:, 5:6], tc1, axis=AXX),
                             reads=[("xn",)], writes=[("s2b",)])
                        p.op("V", lambda e: e.tensor_tensor(rs[:, 4:5], rs[:, 4:5],
                                                            rs[:, 5:6], ALU.add),
                             reads=[("s2a",), ("s2b",)], writes=[("s2a",)])
                p.op("V", lambda e: e.tensor_scalar(rs[:, 4:5], rs[:, 4:5], 1.0 / FH,
                                                    EPS_NORM, ALU.mult, ALU.add),
                     reads=[("s2a",)], writes=[("s2a",)])
                p.op("S", lambda e: e.activation(rs[:, 6:7], rs[:, 4:5], AF.Sqrt),
                     reads=[("s2a",)], writes=[("s2c",)])
                p.op("V", lambda e: e.reciprocal(rs[:, 7:8], rs[:, 6:7]),
                     reads=[("s2c",)], writes=[("s2d",)])
                for ch in range(FH // 512):
                    p.op("V", lambda e, ch=ch: e.tensor_tensor(
                        tc1, h2[:, ch * 512:(ch + 1) * 512],
                        n2b[:, ch * 512:(ch + 1) * 512], ALU.mult),
                        reads=[("h2", ch), ("n2b",)], writes=[("xn",)])
                    if ch == 0:
                        p.op("V", lambda e: e.reduce_max(scal[:, 4:5], tc1, axis=AXX,
                                                         apply_absolute_value=True),
                             reads=[("xn",)], writes=[("am1",)])
                    else:
                        p.op("V", lambda e: e.reduce_max(scal[:, 3:4], tc1, axis=AXX,
                                                         apply_absolute_value=True),
                             reads=[("xn",)], writes=[("amt",)])
                        p.op("V", lambda e: e.tensor_tensor(scal[:, 4:5], scal[:, 4:5],
                                                            scal[:, 3:4], ALU.max),
                             reads=[("am1",), ("amt",)], writes=[("am1",)])
                p.op("V", lambda e: e.tensor_tensor(scal[:, 4:5], scal[:, 4:5], rs[:, 7:8],
                                                    ALU.mult),
                     reads=[("am1",), ("s2d",)], writes=[("am1",)])
                p.op("V", lambda e: e.tensor_scalar(scal[:, 5:6], scal[:, 4:5], 1e-5,
                                                    1.0 / 127.0, ALU.max, ALU.mult),
                     reads=[("am1",)], writes=[("sxi",)])
                p.op("V", lambda e: e.reciprocal(scal[:, 6:7], scal[:, 5:6]),
                     reads=[("sxi",)], writes=[("sx",)])
                p.op("V", lambda e: e.tensor_tensor(scal[:, 6:7], scal[:, 6:7], rs[:, 7:8],
                                                    ALU.mult),
                     reads=[("sx",), ("s2d",)], writes=[("sx",)])
                for ch in range(FH // 512):
                    p.op("V", lambda e, ch=ch: e.scalar_tensor_tensor(
                        tc1, h2[:, ch * 512:(ch + 1) * 512], scal[:, 6:7],
                        n2b[:, ch * 512:(ch + 1) * 512], ALU.mult, ALU.mult),
                        reads=[("h2", ch), ("n2b",), ("sx",)], writes=[("xn",)])
                    p.op("V", lambda e: e.tensor_scalar(tc1, tc1, self.magic[:],
                                                        self.magic[:], ALU.add, ALU.subtract),
                         reads=[("xn",), ("magic",)], writes=[("xn",)])
                    p.op("V", lambda e, ch=ch: e.tensor_scalar(
                        xq2[:, ch * 512:(ch + 1) * 512], tc1, 127.0, -128.0,
                        ALU.min, ALU.max),
                        reads=[("xn",)], writes=[("h2", ch)])
                for ht in range(HT):
                    tb, tkey = self.tbank()
                    p.op("T", lambda e, ht=ht, tb=tb: e.transpose(
                        tb[:, 0:128], xq2[:, ht * 128:(ht + 1) * 128], self.ident[:]),
                        reads=[("h2", ht // 4), ("ident",)], writes=[tkey])
                    p.op("V", lambda e, ht=ht, tb=tb: e.tensor_copy(
                        xq2T[:, ht * 128:(ht + 1) * 128], tb[:, 0:128]),
                        reads=[tkey], writes=[("xq2T", ht)])
                p.op("V", lambda e: e.tensor_tensor(scal[:, 7:8], scal[:, 5:6], sw2i[:],
                                                    ALU.mult),
                     reads=[("sxi",), ("b", "swinv")], writes=[("hsc2",)])
                for ch in range(2):
                    bank, bkey = self.bank()

                    def mm(e, ch=ch, bank=bank):
                        for ht in range(HT):
                            ins = e.matmul(
                                bank[:],
                                xq2T[:, ht * 128:(ht + 1) * 128],
                                w2qT[:, ht * 1024 + ch * 512:ht * 1024 + (ch + 1) * 512],
                                start=(ht == 0), stop=(ht == HT - 1))
                        return ins
                    p.op("T", mm, reads=[("xq2T", ht2) for ht2 in range(HT)]
                         + [("b", "wqT")],
                         writes=[bkey])
                    p.op("V", lambda e, ch=ch, rt=rt, bank=bank: e.scalar_tensor_tensor(
                        rsum[:, ch * 512:(ch + 1) * 512], bank[:], scal[:, 7:8],
                        rt[:, ch * 512:(ch + 1) * 512], ALU.mult, ALU.add),
                        reads=[bkey, ("hsc2",), rk], writes=[("rsum",)])
                if out_dram is None:
                    self.ln_tile(("rsum",), rsum[:], xn, s1, s2, rt, [rk], 1024, EPS_LN,
                                 kq=("xn",))
                else:
                    self.ln_tile(("rsum",), rsum[:], xn, s1, s2, rsum[:], [("rsum",)],
                                 1024, EPS_LN, kq=("xn",))
                    dst = out_dram[out_s, t * 128:(t + 1) * 128, :]
                    p.op("Q", lambda e, dst=dst: e.dma_start(dst, rsum[:]),
                         reads=[("rsum",)], dma=True)
            p.barrier()

def build_l0_attn_program(S):
    nc = bass.Bass("TRN2", target_bir_lowering=False)
    x = nc.dram_tensor("x", [1, S, 1024], F32, kind="ExternalInput").ap()
    w_q = nc.dram_tensor("w_q", [1024, 1024], F32, kind="ExternalInput").ap()
    w_kv = nc.dram_tensor("w_kv", [1024, 1024], F32, kind="ExternalInput").ap()
    w_o = nc.dram_tensor("w_o", [1024, 1024], F32, kind="ExternalInput").ap()
    k_w = nc.dram_tensor("k_w", [1024], F32, kind="ExternalInput").ap()
    out = nc.dram_tensor("out", [1, S, 1024], F32, kind="ExternalOutput").ap()
    with contextlib.ExitStack() as stack:
        prog = Prog(nc, stack)
        k = Kern(nc, prog, stack, S)
        k.init_consts()
        k.load_r(x, 0)
        k.l0_attn(w_q, w_kv, w_o, k_w)
        k.store_r(out, 0)
        prog.drain()
        prog.finalize()
    return nc


def build_l0_ffn_program(S):
    nc = bass.Bass("TRN2", target_bir_lowering=False)
    x = nc.dram_tensor("x", [1, S, 1024], F32, kind="ExternalInput").ap()
    rw = nc.dram_tensor("rw", [8, 1024], F32, kind="ExternalInput").ap()
    w1 = nc.dram_tensor("w1", [1024, 8192], F32, kind="ExternalInput").ap()
    w2 = nc.dram_tensor("w2", [8192, 1024], F32, kind="ExternalInput").ap()
    out = nc.dram_tensor("out", [1, S, 1024], F32, kind="ExternalOutput").ap()
    with contextlib.ExitStack() as stack:
        prog = Prog(nc, stack)
        k = Kern(nc, prog, stack, S)
        k.init_consts()
        k.load_r(x, 0)
        k.l0_ffn(rw, w1, w2)
        k.store_r(out, 0)
        prog.drain()
        prog.finalize()
    return nc


def build_l1_attn_program(S):
    nc = bass.Bass("TRN2", target_bir_lowering=False)
    x = nc.dram_tensor("x", [1, S, 1024], F32, kind="ExternalInput").ap()
    Wq = nc.dram_tensor("Wq", [1024, 1024], F32, kind="ExternalInput").ap()
    Wk = nc.dram_tensor("Wk", [64, 1024], F32, kind="ExternalInput").ap()
    Wv = nc.dram_tensor("Wv", [64, 1024], F32, kind="ExternalInput").ap()
    out = nc.dram_tensor("out", [1, S, 1024], F32, kind="ExternalOutput").ap()
    with contextlib.ExitStack() as stack:
        prog = Prog(nc, stack)
        k = Kern(nc, prog, stack, S)
        k.init_consts()
        k.load_r(x, 0)
        k.l1_attn(Wq, Wk, Wv)
        k.store_r(out, 0)
        prog.drain()
        prog.finalize()
    return nc


def build_l1_ffn_program(S):
    nc = bass.Bass("TRN2", target_bir_lowering=False)
    x = nc.dram_tensor("x", [1, S, 1024], F32, kind="ExternalInput").ap()
    n1 = nc.dram_tensor("n1", [1024], F32, kind="ExternalInput").ap()
    w1 = nc.dram_tensor("w1", [4096, 1024], F32, kind="ExternalInput").ap()
    asc = nc.dram_tensor("asc", [1], F32, kind="ExternalInput").ap()
    abi = nc.dram_tensor("abi", [1], F32, kind="ExternalInput").ap()
    n2 = nc.dram_tensor("n2", [4096], F32, kind="ExternalInput").ap()
    w2 = nc.dram_tensor("w2", [1024, 4096], F32, kind="ExternalInput").ap()
    out = nc.dram_tensor("out", [1, S, 1024], F32, kind="ExternalOutput").ap()
    with contextlib.ExitStack() as stack:
        prog = Prog(nc, stack)
        k = Kern(nc, prog, stack, S)
        k.init_consts()
        k.wquant_pass1(w1, FFN_H, 1024, "a")
        k.wquant_pass1(w2, 1024, FFN_H, "b")
        k.load_r(x, 0)
        k.l1_ffn(n1, w1, asc, abi, n2, w2, out_dram=out, out_s=0)
        prog.drain()
        prog.finalize()
    return nc


_STAGE_WEIGHTS = {
    "l0a": [("w_q", [1024, 1024]), ("w_kv", [1024, 1024]), ("w_o", [1024, 1024]),
            ("k_w", [1024])],
    "l0f": [("rw", [8, 1024]), ("w1", [1024, 8192]), ("w2", [8192, 1024])],
    "l1a": [("Wq", [1024, 1024]), ("Wk", [64, 1024]), ("Wv", [64, 1024])],
    "l1f": [("n1", [1024]), ("w1", [4096, 1024]), ("asc", [1]), ("abi", [1]),
            ("n2", [4096]), ("w2", [1024, 4096])],
}


def build_stage(stage, S, B):
    nc = bass.Bass("TRN2", target_bir_lowering=False)
    x = nc.dram_tensor("x", [B, S, 1024], F32, kind="ExternalInput").ap()
    w = {name: nc.dram_tensor(name, shape, F32, kind="ExternalInput").ap()
         for name, shape in _STAGE_WEIGHTS[stage]}
    out = nc.dram_tensor("out", [B, S, 1024], F32, kind="ExternalOutput").ap()
    with contextlib.ExitStack() as stack:
        prog = Prog(nc, stack)
        k = Kern(nc, prog, stack, S)
        k.init_consts()
        for s in range(B):
            k.load_r(x, s)
            if stage == "l0a":
                k.l0_attn(w["w_q"], w["w_kv"], w["w_o"], w["k_w"])
                k.store_r(out, s)
            elif stage == "l0f":
                k.l0_ffn(w["rw"], w["w1"], w["w2"])
                k.store_r(out, s)
            elif stage == "l1a":
                k.l1_attn(w["Wq"], w["Wk"], w["Wv"])
                k.store_r(out, s)
            else:
                if s == 0:
                    k.wquant_pass1(w["w1"], FFN_H, 1024, "a")
                    k.wquant_pass1(w["w2"], 1024, FFN_H, "b")
                k.l1_ffn(w["n1"], w["w1"], w["asc"], w["abi"], w["n2"], w["w2"],
                         out_dram=out, out_s=s, first_seq=(s == 0))
        prog.drain()
        prog.finalize()
    return nc


def build_fused(S, B):
    nc = bass.Bass("TRN2", target_bir_lowering=False)
    x = nc.dram_tensor("x", [B, S, 1024], F32, kind="ExternalInput").ap()
    w = {}
    for stage, lst in _STAGE_WEIGHTS.items():
        for name, shape in lst:
            w[stage + "_" + name] = nc.dram_tensor(
                stage + "_" + name, shape, F32, kind="ExternalInput").ap()
    out = nc.dram_tensor("out", [B, S, 1024], F32, kind="ExternalOutput").ap()
    with contextlib.ExitStack() as stack:
        prog = Prog(nc, stack)
        k = Kern(nc, prog, stack, S)
        k.init_consts()
        for s in range(B):
            k.load_r(x, s)
            if s == 0:
                k.wquant_pass1(w["l1f_w1"], FFN_H, 1024, "a")
                k.wquant_pass1(w["l1f_w2"], 1024, FFN_H, "b")
            k.l0_attn(w["l0a_w_q"], w["l0a_w_kv"], w["l0a_w_o"], w["l0a_k_w"])
            k.l0_ffn(w["l0f_rw"], w["l0f_w1"], w["l0f_w2"])
            k.l1_attn(w["l1a_Wq"], w["l1a_Wk"], w["l1a_Wv"])
            k.l1_ffn(w["l1f_n1"], w["l1f_w1"], w["l1f_asc"], w["l1f_abi"],
                     w["l1f_n2"], w["l1f_w2"], out_dram=out, out_s=s,
                     first_seq=(s == 0))
        prog.drain()
        prog.finalize()
    return nc


_PROG_CACHE = {}


def _stage_prog(stage, S, B):
    key = (stage, S, B)
    if key not in _PROG_CACHE:
        _PROG_CACHE[key] = build_stage(stage, S, B)
    return _PROG_CACHE[key]


def kernel(**inputs):
    S, B_PER = 2048, 2
    x = np.ascontiguousarray(inputs["L0_attn_x"], dtype=np.float32)
    f32 = lambda a: np.ascontiguousarray(a, dtype=np.float32)
    shards = [x[c * B_PER:(c + 1) * B_PER] for c in range(NCORES)]
    wmap = {
        "l0a_w_q": f32(inputs["L0_attn_w_q"]), "l0a_w_kv": f32(inputs["L0_attn_w_kv"]),
        "l0a_w_o": f32(inputs["L0_attn_w_o"]),
        "l0a_k_w": f32(inputs["L0_attn_k_weights"]).reshape(1024),
        "l0f_rw": f32(inputs["L0_ffn_router_w"]), "l0f_w1": f32(inputs["L0_ffn_w1"]),
        "l0f_w2": f32(inputs["L0_ffn_w2"]),
        "l1a_Wq": f32(inputs["L1_attn_Wq"]), "l1a_Wk": f32(inputs["L1_attn_Wk"]),
        "l1a_Wv": f32(inputs["L1_attn_Wv"]),
        "l1f_n1": f32(inputs["L1_ffn_norm1_w"]), "l1f_w1": f32(inputs["L1_ffn_w1"]),
        "l1f_asc": f32(inputs["L1_ffn_act_scale"]), "l1f_abi": f32(inputs["L1_ffn_act_bias"]),
        "l1f_n2": f32(inputs["L1_ffn_norm2_w"]), "l1f_w2": f32(inputs["L1_ffn_w2"]),
    }
    nc = build_fused(S, B_PER)
    in_maps = [dict(wmap, x=shards[c]) for c in range(NCORES)]
    res = run_bass_kernel_spmd(nc, in_maps, core_ids=list(range(NCORES)))
    return np.concatenate([np.asarray(res.results[c]["out"]) for c in range(NCORES)],
                          axis=0)


# version 38
# speedup vs baseline: 1.0126x; 1.0126x over previous
import contextlib
import numpy as np
import ml_dtypes
import concourse.bass as bass
import concourse.mybir as mybir
from concourse.bass_utils import run_bass_kernel_spmd

F32 = mybir.dt.float32
BF16 = mybir.dt.bfloat16
AF = mybir.ActivationFunctionType
ALU = mybir.AluOpType
AXX = mybir.AxisListType.X

NCORES = 8
D = 1024
DT = D // 128
H = 16
DV = 64
FFN_H = 4096
EPS_LN = 1e-5
EPS_NORM = 1e-8

MAGIC23 = 12582912.0
MAGIC_K = 134217728.0 * (6755399441055744.0 - 12582912.0)
EPS27 = 7.450580596923828e-09


class Prog:

    ENGS = ("T", "V", "S", "P", "Q")
    DMAQ = ("P", "Q")
    NDS = 6

    def __init__(self, nc, stack):
        self.nc = nc
        self.sem = {e: stack.enter_context(nc.semaphore("sem" + e)) for e in self.ENGS}
        self.dsem = {q: [stack.enter_context(nc.semaphore("dma%s%d" % (q, i)))
                         for i in range(self.NDS)] for q in self.DMAQ}
        self.cnt = {e: 0 for e in self.ENGS}
        self.dcnt = {q: [0] * self.NDS for q in self.DMAQ}
        self.drr = {q: 0 for q in self.DMAQ}
        self.ops = {e: [] for e in self.ENGS}
        self.wm = {e: {} for e in self.ENGS}
        self.last_w = {}
        self.readers = {}
        self.pending = set()
        self.n_instr = 0

    def _wait(self, e, sem, c, sid):
        if c <= 0 or self.wm[e].get(sid, 0) >= c:
            return
        self.wm[e][sid] = c
        self.ops[e].append(lambda eng, sem=sem, c=c: eng.wait_ge(sem, c))
        self.n_instr += 1

    def _wait_cover(self, e, cover):
        kind = cover[0]
        if kind == "c":
            _, f, c = cover
            self._wait(e, self.sem[f], c, f)
        else:
            _, q, i, c = cover
            self._wait(e, self.dsem[q][i], c, (q, i))

    def op(self, e, fn, reads=(), writes=(), dma=False, keep=()):
        need = {}

        def _need(cov):
            if cov is None:
                return
            key = cov[:-1]
            if key not in need or need[key][-1] < cov[-1]:
                need[key] = cov
        for k in reads:
            _need(self.last_w.get(k))
        for k in writes:
            _need(self.last_w.get(k))
            for cov in self.readers.get(k, {}).values():
                _need(cov)
        for cov in sorted(need.values(), key=str):
            self._wait_cover(e, cov)
        if dma:
            assert e in self.DMAQ
            i = self.drr[e]
            self.drr[e] = (i + 1) % self.NDS
            prev = self.dcnt[e][i]
            if prev:
                self._wait(e, self.dsem[e][i], prev, (e, i))
            self.dcnt[e][i] = prev + 16
            cover = ("d", e, i, prev + 16)
            sem = self.dsem[e][i]

            def emit(eng, fn=fn, sem=sem):
                fn(eng).then_inc(sem, 16)
        else:
            self.cnt[e] += 1
            cover = ("c", e, self.cnt[e])
            sem = self.sem[e]

            def emit(eng, fn=fn, sem=sem):
                fn(eng).then_inc(sem, 1)

        self.ops[e].append(emit)
        self.n_instr += 1
        for k in reads:
            self.readers.setdefault(k, {})[(e, dma)] = cover
            if k not in keep:
                self.pending.discard(k)
        for k in writes:
            self.last_w[k] = cover
            self.readers[k] = {}
            if isinstance(k, tuple) and k and k[0] in ("pb", "tb"):
                self.pending.add(k)

    def barrier(self):
        covers = [("c", f, self.cnt[f]) for f in self.ENGS]
        covers += [("d", q, i, self.dcnt[q][i]) for q in self.DMAQ for i in range(self.NDS)]
        for e in self.ENGS:
            for cov in covers:
                if cov[-1] > 0:
                    self._wait_cover(e, cov)

    def drain(self):
        self.barrier()

    def finalize(self):
        with self.nc.Block() as block:
            block.tensor(lambda eng: [f(eng) for f in self.ops["T"]])
            block.vector(lambda eng: [f(eng) for f in self.ops["V"]])
            block.scalar(lambda eng: [f(eng) for f in self.ops["S"]])
            block.gpsimd(lambda eng: [f(eng) for f in self.ops["P"]])
            block.sync(lambda eng: [f(eng) for f in self.ops["Q"]])


def _uptri_mask_T():
    m = np.zeros((128, 128), dtype=np.float32)
    for k in range(128):
        m[k, k:] = 1.0
    return m.astype(ml_dtypes.bfloat16)


class Kern:
    def __init__(self, nc, prog, stack, S):
        self.nc = nc
        self.p = prog
        self.stack = stack
        self.S = S
        self.NT = S // 128
        sb = lambda name, shape, dt: stack.enter_context(nc.sbuf_tensor(name, shape, dt))
        ps = lambda name, shape, dt: stack.enter_context(nc.psum_tensor(name, shape, dt))
        NT = self.NT
        self.r = sb("r", [128, NT * 1024], BF16)
        self.ident = sb("ident", [128, 128], BF16)
        self.maskT = sb("maskT", [128, 128], BF16)
        self.magic = sb("magic", [128, 1], F32)
        self.ones_bf = sb("ones_bf", [128, 128], BF16)
        self.onerow = sb("onerow", [1, 128], F32)
        self.identf = sb("identf", [128, 128], F32)
        self.pbd = [ps("pbd%d" % i, [128, 1024], F32) for i in range(2)]
        self.pbs = [ps("pbs%d" % i, [128, 512], F32) for i in range(2)]
        self.pb = [self.pbd[0][:, 0:512], self.pbd[0][:, 512:1024],
                   self.pbd[1][:, 0:512], self.pbd[1][:, 512:1024],
                   self.pbs[0][:, :], self.pbs[1][:, :]]
        self.dummy_ps = self.pbs[1]
        self._pbd_rr = 0
        self.wch = [sb("wchp%d" % i, [128, 512], F32) for i in range(2)]
        self.swi = {t: sb("swi" + t, [128, 1], F32) for t in ("a", "b")}
        self.swv = {t: sb("sw" + t, [128, 1], F32) for t in ("a", "b")}
        self.wacc = sb("wacc", [128, 4], F32)
        self.onef = sb("onef", [128, 1], F32)
        self.wq_dram = {}
        self.tb = [ps("tb%d" % i, [128, 1024], BF16) for i in range(2)]
        self._tb_rr = 0
        self._pb_rr = 0
        self._nid = 0
        self.ident_dram = nc.inline_tensor(np.eye(128, dtype=ml_dtypes.bfloat16), "identc")
        self.identf_dram = nc.inline_tensor(np.eye(128, dtype=np.float32), "identfc")
        self.mask_dram = nc.inline_tensor(_uptri_mask_T(), "maskc")

    def init_consts(self):
        p = self.p
        p.op("Q", lambda e: e.dma_start(self.ident[:], self.ident_dram.ap()),
             writes=[("ident",)], dma=True)
        p.op("Q", lambda e: e.dma_start(self.maskT[:], self.mask_dram.ap()),
             writes=[("maskT",)], dma=True)
        p.op("V", lambda e: e.memset(self.ones_bf[:], 1.0), writes=[("ones_bf",)])
        p.op("V", lambda e: e.memset(self.onerow[:], 1.0), writes=[("onerow",)])
        p.op("Q", lambda e: e.dma_start(self.identf[:], self.identf_dram.ap()),
             writes=[("identf",)], dma=True)
        p.op("V", lambda e: e.memset(self.magic[:], 1.0), writes=[("magic",)])
        p.op("V", lambda e: e.tensor_scalar(self.magic[:], self.magic[:], EPS27, -1.0,
                                            ALU.add, ALU.add),
             reads=[("magic",)], writes=[("magic",)])
        p.op("V", lambda e: e.tensor_scalar(self.magic[:], self.magic[:], MAGIC_K, MAGIC23,
                                            ALU.mult, ALU.add),
             reads=[("magic",)], writes=[("magic",)])

    def _sb(self, st, name, shape, dt):
        self._nid += 1
        return st.enter_context(self.nc.sbuf_tensor("sb%d_%s" % (self._nid, name), shape, dt))

    def tbank(self):
        for _ in range(2):
            b = self._tb_rr
            self._tb_rr = (self._tb_rr + 1) % 2
            if ("tb", b) not in self.p.pending:
                return self.tb[b], ("tb", b)
        raise RuntimeError("both transpose banks pending")

    def warm(self):
        self.p.op("T", lambda e: e.matmul(
            self.dummy_ps[0:64, 0:64], self.ones_bf[0:1, 0:64],
            self.ones_bf[0:1, 0:64], start=True, stop=True),
            reads=[("ones_bf",)], writes=[])

    def bank_pair(self):
        for _ in range(2):
            i = self._pbd_rr
            self._pbd_rr = (self._pbd_rr + 1) % 2
            if (("pb", 2 * i) not in self.p.pending
                    and ("pb", 2 * i + 1) not in self.p.pending):
                return self.pbd[i], ("pb", 2 * i), ("pb", 2 * i + 1)
        raise RuntimeError("both psum pairs pending")

    def bank(self):
        n = len(self.pb)
        for _ in range(n):
            b = self._pb_rr
            self._pb_rr = (self._pb_rr + 1) % n
            if ("pb", b) not in self.p.pending:
                return self.pb[b], ("pb", b)
        raise RuntimeError("all psum banks pending (producer emitted with no "
                           "consumer before next producer)")

    def load_r(self, x_dram, s):
        p, NT = self.p, self.NT
        for t0 in range(0, NT, 2):
            src = x_dram[s, t0 * 128:(t0 + 2) * 128, :].rearrange(
                "(t p) c -> p t c", p=128)
            dst = self.r[:, t0 * 1024:(t0 + 2) * 1024].rearrange(
                "p (t c) -> p t c", c=1024)
            p.op("P", lambda e, src=src, dst=dst: e.dma_start(dst, src),
                 writes=[("r", t0), ("r", t0 + 1)], dma=True)

    def store_r(self, out_dram, s):
        p, NT = self.p, self.NT
        dst = out_dram[s].rearrange("(t p) c -> p t c", p=128)
        src = self.r[:].rearrange("p (t c) -> p t c", c=1024)
        p.op("P", lambda e: e.dma_start(dst, src),
             reads=[("r", t) for t in range(NT)], dma=True)

    def transpose_r(self, rT):
        p, NT, S = self.p, self.NT, self.S
        for dt in range(DT):
            for t in range(NT):
                i = dt * NT + t
                tb, tkey = self.tbank()
                src = self.r[:, t * 1024 + dt * 128: t * 1024 + (dt + 1) * 128]
                p.op("T", lambda e, src=src, tb=tb: e.transpose(tb[:, 0:128], src, self.ident[:]),
                     reads=[("r", t), ("ident",)], writes=[tkey])
                dstc = dt * S + t * 128
                if t % 2 == 0:
                    p.op("V", lambda e, tb=tb, dstc=dstc, rT=rT: e.tensor_copy(
                        rT[:, dstc:dstc + 128], tb[:, 0:128]),
                        reads=[tkey], writes=[("rT", dt, t)])
                else:
                    p.op("S", lambda e, tb=tb, dstc=dstc, rT=rT: e.copy(
                        rT[:, dstc:dstc + 128], tb[:, 0:128]),
                        reads=[tkey], writes=[("rT", dt, t)])

    def ln_tile(self, rsum_key, rsum_ap, sq, s1, s2, out_ap, out_keys, n_feat, eps,
                kq=("lnsq",)):
        p = self.p
        k1, k2 = ("lns1",), ("lns2",)
        nch = n_feat // 512
        for ch in range(nch):
            p.op("V", lambda e, ch=ch: e.bn_stats(
                sq[:, ch * 6:(ch + 1) * 6], rsum_ap[:, ch * 512:(ch + 1) * 512]),
                 reads=[rsum_key], writes=[kq])
        p.op("V", lambda e: e.bn_aggr(s1[:, 0:2], sq[:, 0:6 * nch]),
             reads=[kq], writes=[k1])
        p.op("V", lambda e: e.tensor_scalar_add(s2[:, 0:1], s1[:, 1:2], eps),
             reads=[k1], writes=[k2])
        p.op("S", lambda e: e.activation(s2[:, 1:2], s2[:, 0:1], AF.Sqrt),
             reads=[k2], writes=[k2])
        p.op("V", lambda e: e.reciprocal(s2[:, 2:3], s2[:, 1:2]),
             reads=[k2], writes=[k2])
        p.op("V", lambda e: e.tensor_scalar(out_ap, rsum_ap, s1[:, 0:1], s2[:, 2:3],
                                            ALU.subtract, ALU.mult),
             reads=[rsum_key, k1, k2], writes=out_keys)


    def l0_attn(self, w_q, w_kv, w_o, k_weights):
        nc, p = self.nc, self.p
        S, NT = self.S, self.NT
        with contextlib.ExitStack() as st:
            sb = lambda name, shape, dt: self._sb(st, name, shape, dt)
            rT = sb("rT", [128, DT * S], BF16)
            self.transpose_r(rT)
            qTa = [sb("qTa", [65, 2 * S], BF16) for _ in range(2)]
            kTa = [sb("kTa", [65, 2 * S], BF16) for _ in range(2)]
            kvtok = [sb("kvtok", [128, NT * 2 * 65], BF16) for _ in range(2)]
            wbuf = [sb("wbuf", [128, 2 * DT * 128], BF16) for _ in range(2)]
            kw = [sb("kw", [128, 2], F32) for _ in range(2)]
            kvs = [sb("kvs", [128, S], BF16) for _ in range(2)]
            mneg = [sb("mneg", [128, NT], F32) for _ in range(2)]
            mnegb = [sb("mnegb", [128, NT], BF16) for _ in range(2)]
            mrow = [sb("mrow", [NT, 128], BF16) for _ in range(2)]
            eT = sb("eT", [128, NT * 512], BF16)
            yT = sb("yT", [128, DT * S], BF16)
            rec = sb("rec", [1, 512], F32)
            recb = sb("recb", [1, 512], BF16)
            bcd = sb("bcd", [64, 512], F32)
            wo = sb("wo", [128, DT * 1024], BF16)
            rsum = sb("rsum", [128, 1024], F32)
            sq = sb("sq", [128, 1024], F32)
            s1 = sb("s1", [128, 4], F32)
            s2 = sb("s2", [128, 4], F32)

            p.op("P", lambda e: e.dma_start(
                wo[:].rearrange("p (dt c) -> p dt c", c=1024),
                w_o.rearrange("(dt p) c -> p dt c", p=128)),
                writes=[("wo",)], dma=True)
            for pr in range(2):
                p.op("V", lambda e, pr=pr: e.memset(kTa[pr][64:65, :], 1.0),
                     writes=[("kTa_ones", pr)])
                p.op("V", lambda e, pr=pr: e.memset(
                    kvtok[pr][:].rearrange("p (t h c) -> p t h c", h=2, c=65)[:, :, :, 64:65],
                    1.0),
                    writes=[("kvtok_ones", pr)])

            def prep(g):
                pr = g % 2
                qT_, kT_, kv_, wb_, kw_, kvs_ = (qTa[pr], kTa[pr], kvtok[pr],
                                                 wbuf[pr], kw[pr], kvs[pr])
                p.op("P", lambda e: e.dma_start(
                    wb_[:, 0:DT * 128].rearrange("p (dt c) -> p dt c", c=128),
                    w_q[:, g * 128:(g + 1) * 128].rearrange("(dt p) c -> p dt c", p=128)),
                    writes=[("wbufq", pr)], dma=True)
                yield
                p.op("P", lambda e: e.dma_start(
                    wb_[:, DT * 128:].rearrange("p (dt c) -> p dt c", c=128),
                    w_kv[:, g * 128:(g + 1) * 128].rearrange("(dt p) c -> p dt c", p=128)),
                    writes=[("wbufk", pr)], dma=True)
                yield
                p.op("P", lambda e: e.dma_start(
                    kw_[:, 0:1], k_weights[g * 128:(g + 1) * 128, None]),
                    writes=[("kw", pr)], dma=True)
                yield
                p.op("V", lambda e: e.tensor_scalar_add(kw_[:, 1:2], kw_[:, 0:1], 1.0),
                     reads=[("kw", pr)], writes=[("kw1", pr)])
                yield
                for which in range(2):
                    wk = ("wbufq", pr) if which == 0 else ("wbufk", pr)
                    for c in range(S // 512):
                        bank, bkey = self.bank()
                        rds = [wk] + [("rT", dt, tt) for dt in range(DT)
                                      for tt in range(c * 4, c * 4 + 4)]

                        def mm(e, which=which, c=c, bank=bank, wb_=wb_):
                            for dt in range(DT):
                                ins = e.matmul(
                                    bank[:],
                                    wb_[:, which * DT * 128 + dt * 128:
                                        which * DT * 128 + (dt + 1) * 128],
                                    rT[:, dt * S + c * 512: dt * S + (c + 1) * 512],
                                    start=(dt == 0), stop=(dt == DT - 1))
                            return ins
                        p.op("T", mm, reads=rds, writes=[bkey])
                        yield
                        if which == 0:
                            for hh in range(2):
                                p.op("V", lambda e, hh=hh, c=c, bank=bank, qT_=qT_:
                                     e.tensor_scalar_mul(
                                         qT_[0:64, hh * S + c * 512: hh * S + (c + 1) * 512],
                                         bank[hh * 64:hh * 64 + 64, :], 1.0 / 32.0),
                                     reads=[bkey], writes=[("qTa", pr, hh, c)])
                                yield
                        else:
                            p.op("V", lambda e, c=c, bank=bank, kvs_=kvs_: e.tensor_copy(
                                kvs_[:, c * 512:(c + 1) * 512], bank[:]),
                                reads=[bkey], writes=[("kvs", pr, c)])
                            yield
                            for hh in range(2):
                                p.op("V", lambda e, hh=hh, c=c, kT_=kT_, kvs_=kvs_, kw_=kw_:
                                     e.tensor_scalar(
                                         kT_[0:64, hh * S + c * 512: hh * S + (c + 1) * 512],
                                         kvs_[hh * 64:hh * 64 + 64, c * 512:(c + 1) * 512],
                                         kw_[hh * 64:hh * 64 + 64, 1:2], None, ALU.mult),
                                     reads=[("kvs", pr, c), ("kw1", pr)],
                                     writes=[("kTa", pr, hh, c)])
                                yield
                for hh in range(2):
                    for t in range(NT):
                        tb, tkey = self.tbank()
                        p.op("T", lambda e, hh=hh, t=t, tb=tb, kvs_=kvs_: e.transpose(
                            tb[:, 0:64], kvs_[hh * 64:hh * 64 + 64, t * 128:(t + 1) * 128],
                            self.ident[hh * 64:hh * 64 + 64, hh * 64:hh * 64 + 64]),
                            reads=[("kvs", pr, t // 4), ("ident",)], writes=[tkey])
                        yield
                        dst = kv_[:, (t * 2 + hh) * 65: (t * 2 + hh) * 65 + 64]
                        p.op("V", lambda e, dst=dst, tb=tb: e.tensor_copy(dst, tb[:, 0:64]),
                             reads=[tkey], writes=[("kvtok", pr, hh, t)])
                        yield
                for hh in range(2):
                    for qt in range(NT):
                        c = qt // 4
                        cols = min(512, (qt + 1) * 128 - c * 512)
                        bank, bkey = self.bank()
                        p.op("T", lambda e, hh=hh, qt=qt, c=c, cols=cols, bank=bank,
                             qT_=qT_, kT_=kT_:
                             e.matmul(bank[:, 0:cols],
                                      qT_[0:64, hh * S + qt * 128:hh * S + (qt + 1) * 128],
                                      kT_[0:64, hh * S + c * 512:hh * S + c * 512 + cols],
                                      start=True, stop=True),
                             reads=[("qTa", pr, hh, qt // 4), ("kTa", pr, hh, c)],
                             writes=[bkey])
                        yield
                        p.op("V", lambda e, qt=qt, cols=cols, bank=bank, mn=mneg[pr]:
                             e.reduce_max(mn[:, qt:qt + 1], bank[:, 0:cols],
                                          axis=AXX, negate=True),
                             reads=[bkey], writes=[("mneg", pr, qt)])
                        yield
                    p.op("V", lambda e, mn=mneg[pr], mb=mnegb[pr]: e.tensor_copy(mb[:], mn[:]),
                         reads=[("mneg", pr, qt2) for qt2 in range(NT)],
                         writes=[("mnegb", pr)])
                    yield
                    mtb, mtkey = self.tbank()
                    p.op("T", lambda e, mb=mnegb[pr], mtb=mtb: e.transpose(
                        mtb[0:NT, 0:128], mb[:], self.ident[:]),
                        reads=[("mnegb", pr), ("ident",)], writes=[mtkey])
                    yield
                    p.op("V", lambda e, mr=mrow[pr], mtb=mtb: e.tensor_copy(
                        mr[:], mtb[0:NT, 0:128]),
                        reads=[mtkey], writes=[("mrow", pr)])
                    yield
                    p.op("P", lambda e, hh=hh, qT_=qT_, mr=mrow[pr]: e.dma_start(
                        qT_[64:65, hh * S:(hh + 1) * S].rearrange("o (t c) -> o t c", c=128),
                        mr[:, :]),
                        reads=[("mrow", pr)], writes=[("qTa64", pr, hh)], dma=True)
                    yield

            def c3(g):
                pr = g % 2
                qT_, kT_, kv_ = qTa[pr], kTa[pr], kvtok[pr]
                for hh in range(2):
                    h = 2 * g + hh
                    for c in range(S // 512):
                        ktmax = min(NT, c * 4 + 4)
                        ekeys = []
                        kt = 0
                        while kt < ktmax:
                            q0 = max(0, kt * 128 - c * 512)
                            bank, bkey = self.bank()
                            p.op("T", lambda e, hh=hh, c=c, kt=kt, q0=q0, bank=bank,
                                 qT_=qT_, kT_=kT_:
                                 e.matmul(bank[:, q0:512],
                                          kT_[:, hh * S + kt * 128:hh * S + (kt + 1) * 128],
                                          qT_[:, hh * S + c * 512 + q0:hh * S + (c + 1) * 512],
                                          start=True, stop=True),
                                 reads=[("kTa", pr, hh, kt // 4), ("kTa_ones", pr),
                                        ("qTa", pr, hh, c), ("qTa64", pr, hh)],
                                 writes=[bkey])
                            yield
                            ek = ("eT", kt)
                            p.op("S", lambda e, kt=kt, q0=q0, bank=bank: e.activation(
                                eT[:, kt * 512 + q0:(kt + 1) * 512], bank[:, q0:512], AF.Exp),
                                reads=[bkey], writes=[ek])
                            yield
                            if kt * 128 >= c * 512:
                                p.op("P", lambda e, kt=kt, q0=q0: e.tensor_tensor(
                                    eT[:, kt * 512 + q0:kt * 512 + q0 + 128],
                                    eT[:, kt * 512 + q0:kt * 512 + q0 + 128],
                                    self.maskT[:], ALU.mult),
                                    reads=[ek, ("maskT",)], writes=[ek])
                                yield
                            ekeys.append(ek)
                            kt += 1
                        bank, bkey = self.bank()

                        def av(e, hh=hh, c=c, ktmax=ktmax, bank=bank, kv_=kv_):
                            for kt in range(ktmax):
                                q0 = max(0, kt * 128 - c * 512)
                                ins = e.matmul(
                                    bank[0:65, q0:512],
                                    kv_[:, (kt * 2 + hh) * 65:(kt * 2 + hh) * 65 + 65],
                                    eT[:, kt * 512 + q0:(kt + 1) * 512],
                                    start=(kt == 0), stop=(kt == ktmax - 1))
                            return ins
                        p.op("T", av,
                             reads=ekeys + [("kvtok", pr, hh, kt2) for kt2 in range(ktmax)]
                             + [("kvtok_ones", pr)],
                             writes=[bkey])
                        yield
                        p.op("S", lambda e, bank=bank: e.activation(rec[:], bank[64:65, :],
                                                                    AF.Ln),
                             reads=[bkey], writes=[("rec",)], keep=[bkey])
                        yield
                        p.op("S", lambda e: e.activation(recb[:], rec[:], AF.Exp, scale=-1.0),
                             reads=[("rec",)], writes=[("recb",)])
                        yield
                        bank2, bkey2 = self.bank()
                        p.op("T", lambda e, bank2=bank2: e.matmul(
                            bank2[0:64, :], self.ones_bf[0:1, 0:64], recb[:],
                            start=True, stop=True),
                            reads=[("recb",), ("ones_bf",)], writes=[bkey2])
                        yield
                        p.op("V", lambda e, bank2=bank2: e.tensor_copy(bcd[:], bank2[0:64, :]),
                             reads=[bkey2], writes=[("bcd",)])
                        yield
                        dt_, row = divmod(h * 64, 128)
                        p.op("V", lambda e, c=c, dt_=dt_, row=row, bank=bank: e.tensor_tensor(
                            yT[row:row + 64, dt_ * S + c * 512:dt_ * S + (c + 1) * 512],
                            bank[0:64, :], bcd[:], ALU.mult),
                            reads=[bkey, ("bcd",)], writes=[("yT", h, c)])
                        yield

            def run(gen):
                for _ in gen:
                    pass

            def interleave(main, filler, ratio=1):
                while True:
                    done_main = False
                    for _ in range(ratio):
                        try:
                            next(main)
                        except StopIteration:
                            done_main = True
                            break
                    if filler is not None:
                        try:
                            next(filler)
                        except StopIteration:
                            filler = None
                    if done_main:
                        break
                if filler is not None:
                    run(filler)

            NG = H // 2
            run(prep(0))
            for g in range(NG):
                interleave(c3(g), prep(g + 1) if g + 1 < NG else None, ratio=2)

            for t in range(NT):
                okeys, banks = [], []
                for half in range(2):
                    bank, bkey = self.bank()

                    def mm(e, t=t, half=half, bank=bank):
                        for dt in range(DT):
                            ins = e.matmul(
                                bank[:],
                                yT[:, dt * S + t * 128:dt * S + (t + 1) * 128],
                                wo[:, dt * 1024 + half * 512:dt * 1024 + (half + 1) * 512],
                                start=(dt == 0), stop=(dt == DT - 1))
                        return ins
                    p.op("T", mm,
                         reads=[("yT", hh2, t // 4) for hh2 in range(H)] + [("wo",)],
                         writes=[bkey])
                    okeys.append(bkey)
                    banks.append(bank)
                rkey = ("rsum",)
                for half in range(2):
                    p.op("V", lambda e, t=t, half=half, bank=banks[half]: e.tensor_tensor(
                        rsum[:, half * 512:(half + 1) * 512], bank[:],
                        self.r[:, t * 1024 + half * 512:t * 1024 + (half + 1) * 512],
                        ALU.add),
                        reads=[okeys[half], ("r", t)],
                        writes=[rkey])
                self.ln_tile(rkey, rsum[:], sq, s1, s2,
                             self.r[:, t * 1024:(t + 1) * 1024], [("r", t)], 1024, EPS_LN)
            p.barrier()


    def l0_ffn(self, router_w, w1, w2):
        nc, p = self.nc, self.p
        S, NT = self.S, self.NT
        with contextlib.ExitStack() as st:
            sb = lambda name, shape, dt: self._sb(st, name, shape, dt)
            rT = sb("rT", [128, DT * S], BF16)
            self.transpose_r(rT)
            rw = sb("rw", [8, 1024], BF16)
            rwT = sb("rwT", [128, DT * 8], BF16)
            cw = sb("cw", [128, NT * 8], F32)
            pr = sb("pr", [128, 8], F32)
            t8 = sb("t8", [128, 8], F32)
            g8 = sb("g8", [128, 8], F32)
            c1 = sb("c1", [128, 6], F32)
            we1 = [sb("we1_%d" % i, [128, DT * 1024], BF16) for i in range(2)]
            we2 = [sb("we2_%d" % i, [128, DT * 1024], BF16) for i in range(2)]
            aT2 = [sb("aT", [128, DT * (S // 2)], BF16) for _ in range(2)]
            rlu = sb("rlu", [128, 512], F32)
            rsum = sb("rsum", [128, (NT // 2) * 1024], F32)
            sq = sb("sq", [128, 1024], F32)
            s1 = sb("s1", [128, 4], F32)
            s2 = sb("s2", [128, 4], F32)

            p.op("P", lambda e: e.dma_start(rw[:], router_w), writes=[("rw",)], dma=True)
            for dt in range(DT):
                tb, tkey = self.tbank()
                p.op("T", lambda e, dt=dt, tb=tb: e.transpose(
                    tb[0:128, 0:8], rw[0:8, dt * 128:(dt + 1) * 128], self.ident[0:8, 0:8]),
                    reads=[("rw",), ("ident",)], writes=[tkey])
                p.op("V", lambda e, dt=dt, tb=tb: e.tensor_copy(
                    rwT[:, dt * 8:(dt + 1) * 8], tb[0:128, 0:8]),
                    reads=[tkey], writes=[("rwT",)])
            for t in range(NT):
                bank, bkey = self.bank()

                def mm(e, t=t, bank=bank):
                    for dt in range(DT):
                        ins = e.matmul(bank[:, 0:8],
                                       rT[:, dt * S + t * 128:dt * S + (t + 1) * 128],
                                       rwT[:, dt * 8:(dt + 1) * 8],
                                       start=(dt == 0), stop=(dt == DT - 1))
                    return ins
                p.op("T", mm, reads=[("rwT",)] + [("rT", dt, t) for dt in range(DT)],
                     writes=[bkey])
                p.op("S", lambda e, bank=bank: e.activation(pr[:], bank[:, 0:8], AF.Exp,
                                                            scale=-1.0),
                     reads=[bkey], writes=[("pr",)])
                p.op("V", lambda e: e.tensor_scalar_add(pr[:], pr[:], 1.0),
                     reads=[("pr",)], writes=[("pr",)])
                p.op("V", lambda e: e.reciprocal(pr[:], pr[:]),
                     reads=[("pr",)], writes=[("pr",)])
                p.op("V", lambda e: e.reduce_max(c1[:, 0:1], pr[:], axis=AXX),
                     reads=[("pr",)], writes=[("c1",)])
                p.op("V", lambda e: e.tensor_scalar(t8[:], pr[:], c1[:, 0:1], None,
                                                    ALU.subtract),
                     reads=[("pr",), ("c1",)], writes=[("t8",)])
                p.op("V", lambda e: e.tensor_single_scalar(g8[:], t8[:], 0.0, ALU.is_equal),
                     reads=[("t8",)], writes=[("g8",)])
                p.op("V", lambda e: e.scalar_tensor_tensor(t8[:], g8[:], -1e30, t8[:],
                                                           ALU.mult, ALU.add),
                     reads=[("t8",), ("g8",)], writes=[("t8",)])
                p.op("V", lambda e: e.reduce_max(c1[:, 1:2], t8[:], axis=AXX),
                     reads=[("t8",)], writes=[("c1b",)])
                p.op("V", lambda e: e.tensor_tensor(c1[:, 2:3], c1[:, 1:2], c1[:, 0:1],
                                                    ALU.add),
                     reads=[("c1",), ("c1b",)], writes=[("c1c",)])
                p.op("V", lambda e: e.tensor_scalar(c1[:, 3:4], c1[:, 2:3], c1[:, 0:1],
                                                    1e-20, ALU.add, ALU.add),
                     reads=[("c1",), ("c1c",)], writes=[("c1d",)])
                p.op("V", lambda e: e.reciprocal(c1[:, 4:5], c1[:, 3:4]),
                     reads=[("c1d",)], writes=[("c1e",)])
                p.op("V", lambda e: e.tensor_scalar(g8[:], pr[:], c1[:, 2:3], None,
                                                    ALU.is_ge),
                     reads=[("pr",), ("c1c",)], writes=[("g8",)])
                p.op("V", lambda e: e.tensor_tensor(t8[:], pr[:], g8[:], ALU.mult),
                     reads=[("pr",), ("g8",)], writes=[("t8",)])
                p.op("V", lambda e, t=t: e.tensor_scalar(cw[:, t * 8:(t + 1) * 8], t8[:],
                                                         c1[:, 4:5], None, ALU.mult),
                     reads=[("t8",), ("c1e",)], writes=[("cw", t)])

            HS = S // 2
            HT = NT // 2
            for half in range(2):
                for t in range(HT):
                    tg = half * HT + t
                    p.op("V", lambda e, t=t, tg=tg: e.tensor_copy(
                        rsum[:, t * 1024:(t + 1) * 1024],
                        self.r[:, tg * 1024:(tg + 1) * 1024]),
                        reads=[("r", tg)], writes=[("rsum", t)])
                for ex in range(8):
                    wb1, wb2 = we1[ex % 2], we2[ex % 2]
                    k1, k2 = ("we1", ex % 2), ("we2", ex % 2)
                    aT = aT2[ex % 2]
                    aki = ex % 2
                    for dh in range(2):
                        p.op("P", lambda e, ex=ex, wb1=wb1, dh=dh: e.dma_start(
                            wb1[:, dh * 4096:(dh + 1) * 4096].rearrange(
                                "p (dt c) -> p dt c", c=1024),
                            w1[dh * 512:(dh + 1) * 512,
                               ex * 1024:(ex + 1) * 1024].rearrange(
                                "(dt p) c -> p dt c", p=128)),
                            writes=[k1], dma=True)
                        p.op("P", lambda e, ex=ex, wb2=wb2, dh=dh: e.dma_start(
                            wb2[:, dh * 4096:(dh + 1) * 4096].rearrange(
                                "p (dt c) -> p dt c", c=1024),
                            w2[ex * 1024 + dh * 512:ex * 1024 + (dh + 1) * 512, :].rearrange(
                                "(dt p) c -> p dt c", p=128)),
                            writes=[k2], dma=True)
                    for wf in range(DT):
                        for c in range(HS // 512):
                            bank, bkey = self.bank()
                            cg = half * (HS // 512) + c

                            def mm(e, wf=wf, cg=cg, bank=bank, wb1=wb1):
                                for dt in range(DT):
                                    ins = e.matmul(
                                        bank[:],
                                        wb1[:, dt * 1024 + wf * 128:dt * 1024 + wf * 128 + 128],
                                        rT[:, dt * S + cg * 512:dt * S + (cg + 1) * 512],
                                        start=(dt == 0), stop=(dt == DT - 1))
                                return ins
                            p.op("T", mm, reads=[k1] + [("rT", dt, tt) for dt in range(DT)
                                                        for tt in range(cg * 4, cg * 4 + 4)],
                                 writes=[bkey])
                            p.op("S", lambda e, bank=bank: e.activation(
                                rlu[:], bank[:], AF.Relu),
                                reads=[bkey], writes=[("rlu",)])
                            p.op("S", lambda e, wf=wf, c=c, aT=aT: e.activation(
                                aT[:, wf * HS + c * 512:wf * HS + (c + 1) * 512],
                                rlu[:], AF.Square),
                                reads=[("rlu",)], writes=[("aT", aki, wf, c)])
                    for t in range(HT):
                        for ch in range(2):
                            bank, bkey = self.bank()

                            def mm(e, t=t, ch=ch, bank=bank, wb2=wb2, aT=aT):
                                for wf in range(DT):
                                    ins = e.matmul(
                                        bank[:],
                                        aT[:, wf * HS + t * 128:wf * HS + (t + 1) * 128],
                                        wb2[:, wf * 1024 + ch * 512:wf * 1024 + (ch + 1) * 512],
                                        start=(wf == 0), stop=(wf == DT - 1))
                                return ins
                            p.op("T", mm, reads=[k2] + [("aT", aki, wf, t // 4)
                                                        for wf in range(DT)],
                                 writes=[bkey])
                            tg = half * HT + t
                            p.op("V", lambda e, t=t, ch=ch, tg=tg, ex=ex, bank=bank:
                                 e.scalar_tensor_tensor(
                                     rsum[:, t * 1024 + ch * 512:t * 1024 + (ch + 1) * 512],
                                     bank[:], cw[:, tg * 8 + ex:tg * 8 + ex + 1],
                                     rsum[:, t * 1024 + ch * 512:t * 1024 + (ch + 1) * 512],
                                     ALU.mult, ALU.add),
                                 reads=[bkey, ("cw", tg), ("rsum", t)], writes=[("rsum", t)])
                for t in range(HT):
                    tg = half * HT + t
                    self.ln_tile(("rsum", t), rsum[:, t * 1024:(t + 1) * 1024], sq, s1, s2,
                                 self.r[:, tg * 1024:(tg + 1) * 1024], [("r", tg)],
                                 1024, EPS_LN)
            p.barrier()


    def l1_attn(self, Wq, Wk, Wv):
        nc, p = self.nc, self.p
        S, NT = self.S, self.NT
        with contextlib.ExitStack() as st:
            sb = lambda name, shape, dt: self._sb(st, name, shape, dt)
            rT = sb("rT", [128, DT * S], BF16)
            self.transpose_r(rT)
            rsum = sb("rsumf", [128, NT * 1024], BF16)
            wqT = sb("wqT", [128, DT * 1024], BF16)
            wkvT = sb("wkvT", [128, DT * 128], BF16)
            qTa2 = [sb("qTa", [65, 2 * S], BF16) for _ in range(2)]
            kTa = sb("kTa", [65, S], BF16)
            va = sb("va", [128, NT * 65], BF16)
            mneg2 = [sb("mneg", [128, NT], F32) for _ in range(2)]
            mnegb2 = [sb("mnegb", [128, NT], BF16) for _ in range(2)]
            mrow2 = [sb("mrow", [NT, 128], BF16) for _ in range(2)]
            yn = sb("yn", [64, 512], BF16)
            rec = sb("rec", [1, 512], F32)
            recb = sb("recb", [1, 512], BF16)
            bcd = sb("bcd", [64, 512], F32)
            sq = sb("sq", [128, 1024], F32)
            s1 = sb("s1", [128, 4], F32)
            s2 = sb("s2", [128, 4], F32)

            wst = st.enter_context(contextlib.ExitStack())
            wraw = self._sb(wst, "wraw", [128, DT * 1024], BF16)
            p.op("P", lambda e: e.dma_start(
                wraw[:].rearrange("p (ft c) -> p ft c", c=1024),
                Wq.rearrange("(ft p) c -> p ft c", p=128)),
                writes=[("wraw",)], dma=True)
            for ft in range(DT):
                for dt in range(DT):
                    i = ft * DT + dt
                    tb, tkey = self.tbank()
                    p.op("T", lambda e, ft=ft, dt=dt, tb=tb: e.transpose(
                        tb[:, 0:128], wraw[:, ft * 1024 + dt * 128:ft * 1024 + (dt + 1) * 128],
                        self.ident[:]),
                        reads=[("wraw",), ("ident",)], writes=[tkey])
                    p.op("V" if i % 2 == 0 else "S",
                         lambda e, ft=ft, dt=dt, tb=tb, m=(i % 2): (
                             e.tensor_copy if m == 0 else e.copy)(
                             wqT[:, dt * 1024 + ft * 128:dt * 1024 + (ft + 1) * 128],
                             tb[:, 0:128]),
                         reads=[tkey], writes=[("wqT", dt)])
            p.op("P", lambda e: e.dma_start(
                wraw[0:64, 0:1024].rearrange("p (dt c) -> p dt c", c=128),
                Wk.rearrange("p (dt c) -> p dt c", c=128)),
                writes=[("wraw",), ("wrawk",)], dma=True)
            p.op("P", lambda e: e.dma_start(
                wraw[64:128, 0:1024].rearrange("p (dt c) -> p dt c", c=128),
                Wv.rearrange("p (dt c) -> p dt c", c=128)),
                writes=[("wraw",), ("wrawv",)], dma=True)
            for dt in range(DT):
                tb, tkey = self.tbank()
                p.op("T", lambda e, dt=dt, tb=tb: e.transpose(
                    tb[:, 0:128], wraw[:, dt * 128:(dt + 1) * 128], self.ident[:]),
                    reads=[("wrawk",), ("wrawv",), ("ident",)], writes=[tkey])
                p.op("V", lambda e, dt=dt, tb=tb: e.tensor_copy(
                    wkvT[:, dt * 128:(dt + 1) * 128], tb[:, 0:128]),
                    reads=[tkey], writes=[("wkvT", dt)])
            p.barrier()
            wst.close()
            eT2 = [sb("eT", [128, NT * 512], BF16) for _ in range(2)]

            p.op("V", lambda e: e.memset(kTa[64:65, :], 1.0), writes=[("kTa_ones",)])
            p.op("V", lambda e: e.memset(
                va[:].rearrange("p (t c) -> p t c", c=65)[:, :, 64:65], 1.0),
                writes=[("va_ones",)])
            for c in range(S // 512):
                bank, bkey = self.bank()

                def mm(e, c=c, bank=bank):
                    for dt in range(DT):
                        ins = e.matmul(bank[0:64, :],
                                       wkvT[:, dt * 128:dt * 128 + 64],
                                       rT[:, dt * S + c * 512:dt * S + (c + 1) * 512],
                                       start=(dt == 0), stop=(dt == DT - 1))
                    return ins
                p.op("T", mm, reads=[("wkvT", dt2) for dt2 in range(DT)]
                     + [("rT", dt2, tt) for dt2 in range(DT) for tt in range(c * 4, c * 4 + 4)],
                     writes=[bkey])
                p.op("V", lambda e, c=c, bank=bank: e.tensor_copy(
                    kTa[0:64, c * 512:(c + 1) * 512], bank[0:64, :]),
                    reads=[bkey], writes=[("kTa", c)])
            for t in range(NT):
                bank, bkey = self.bank()

                def mm(e, t=t, bank=bank):
                    for dt in range(DT):
                        ins = e.matmul(bank[:, 0:64],
                                       rT[:, dt * S + t * 128:dt * S + (t + 1) * 128],
                                       wkvT[:, dt * 128 + 64:dt * 128 + 128],
                                       start=(dt == 0), stop=(dt == DT - 1))
                    return ins
                p.op("T", mm, reads=[("wkvT", dt2) for dt2 in range(DT)]
                     + [("rT", dt2, t) for dt2 in range(DT)],
                     writes=[bkey])
                p.op("V", lambda e, t=t, bank=bank: e.tensor_copy(
                    va[:, t * 65:t * 65 + 64], bank[:, 0:64]),
                    reads=[bkey], writes=[("va", t)])

            def prep(g):
                pr = g % 2
                qTa, mneg, mnegb, mrow = qTa2[pr], mneg2[pr], mnegb2[pr], mrow2[pr]
                for c in range(S // 512):
                    bank, bkey = self.bank()

                    def mm(e, g=g, c=c, bank=bank):
                        for dt in range(DT):
                            ins = e.matmul(
                                bank[:],
                                wqT[:, dt * 1024 + g * 128:dt * 1024 + (g + 1) * 128],
                                rT[:, dt * S + c * 512:dt * S + (c + 1) * 512],
                                start=(dt == 0), stop=(dt == DT - 1))
                        return ins
                    p.op("T", mm, reads=[("wqT", dt2) for dt2 in range(DT)]
                         + [("rT", dt2, tt) for dt2 in range(DT)
                            for tt in range(c * 4, c * 4 + 4)],
                         writes=[bkey])
                    yield
                    for hh in range(2):
                        p.op("V", lambda e, hh=hh, c=c, bank=bank, qTa=qTa: e.tensor_scalar_mul(
                            qTa[0:64, hh * S + c * 512:hh * S + (c + 1) * 512],
                            bank[hh * 64:hh * 64 + 64, :], 0.125),
                            reads=[bkey], writes=[("qTa", pr, hh, c)])
                        yield
                for hh in range(2):
                    for qt in range(NT):
                        c = qt // 4
                        bank, bkey = self.bank()
                        p.op("T", lambda e, hh=hh, qt=qt, c=c, bank=bank, qTa=qTa: e.matmul(
                            bank[:],
                            qTa[0:64, hh * S + qt * 128:hh * S + (qt + 1) * 128],
                            kTa[0:64, c * 512:(c + 1) * 512],
                            start=True, stop=True),
                            reads=[("qTa", pr, hh, qt // 4), ("kTa", c)], writes=[bkey])
                        yield
                        p.op("V", lambda e, qt=qt, bank=bank, mneg=mneg: e.reduce_max(
                            mneg[:, qt:qt + 1], bank[:], axis=AXX, negate=True),
                            reads=[bkey], writes=[("mneg", pr, qt)])
                        yield
                    p.op("V", lambda e, mneg=mneg, mnegb=mnegb: e.tensor_copy(mnegb[:], mneg[:]),
                         reads=[("mneg", pr, qt2) for qt2 in range(NT)],
                         writes=[("mnegb", pr)])
                    yield
                    mtb, mtkey = self.tbank()
                    p.op("T", lambda e, mnegb=mnegb, mtb=mtb: e.transpose(
                        mtb[0:NT, 0:128], mnegb[:], self.ident[:]),
                        reads=[("mnegb", pr), ("ident",)], writes=[mtkey])
                    yield
                    p.op("V", lambda e, mrow=mrow, mtb=mtb: e.tensor_copy(
                        mrow[:], mtb[0:NT, 0:128]),
                        reads=[mtkey], writes=[("mrow", pr)])
                    yield
                    p.op("P", lambda e, hh=hh, qTa=qTa, mrow=mrow: e.dma_start(
                        qTa[64:65, hh * S:(hh + 1) * S].rearrange("o (t c) -> o t c", c=128),
                        mrow[:, :]),
                        reads=[("mrow", pr)], writes=[("qTa64", pr, hh)], dma=True)
                    yield

            def c3(g):
                pr = g % 2
                qTa = qTa2[pr]
                for hh in range(2):
                    h = 2 * g + hh
                    for c in range(S // 512):
                        eT = eT2[c % 2]
                        eki = c % 2
                        ekeys = []
                        for kt in range(NT):
                            bank, bkey = self.bank()
                            p.op("T", lambda e, hh=hh, c=c, kt=kt, bank=bank, qTa=qTa:
                                 e.matmul(
                                     bank[:, :],
                                     kTa[:, kt * 128:(kt + 1) * 128],
                                     qTa[:, hh * S + c * 512:hh * S + (c + 1) * 512],
                                     start=True, stop=True),
                                 reads=[("kTa", kt // 4), ("kTa_ones",),
                                        ("qTa", pr, hh, c), ("qTa64", pr, hh)],
                                 writes=[bkey])
                            yield
                            ek = ("eT", eki, kt)
                            p.op("S", lambda e, kt=kt, bank=bank, eT=eT: e.activation(
                                eT[:, kt * 512:(kt + 1) * 512], bank[:, :], AF.Exp),
                                reads=[bkey], writes=[ek])
                            yield
                            ekeys.append(ek)
                        bank, bkey = self.bank()

                        def av(e, bank=bank, eT=eT):
                            for kt in range(NT):
                                ins = e.matmul(
                                    bank[0:65, :],
                                    va[:, kt * 65:(kt + 1) * 65],
                                    eT[:, kt * 512:(kt + 1) * 512],
                                    start=(kt == 0), stop=(kt == NT - 1))
                            return ins
                        p.op("T", av, reads=ekeys
                             + [("va", kt2) for kt2 in range(NT)] + [("va_ones",)],
                             writes=[bkey])
                        yield
                        p.op("S", lambda e, bank=bank: e.activation(rec[:], bank[64:65, :],
                                                                    AF.Ln),
                             reads=[bkey], writes=[("rec",)], keep=[bkey])
                        yield
                        p.op("S", lambda e: e.activation(recb[:], rec[:], AF.Exp, scale=-1.0),
                             reads=[("rec",)], writes=[("recb",)])
                        yield
                        bank2, bkey2 = self.bank()
                        p.op("T", lambda e, bank2=bank2: e.matmul(
                            bank2[0:64, :], self.ones_bf[0:1, 0:64], recb[:],
                            start=True, stop=True),
                            reads=[("recb",), ("ones_bf",)], writes=[bkey2])
                        yield
                        p.op("V", lambda e, bank2=bank2: e.tensor_copy(bcd[:], bank2[0:64, :]),
                             reads=[bkey2], writes=[("bcd",)])
                        yield
                        p.op("V", lambda e, bank=bank: e.tensor_tensor(
                            yn[:], bank[0:64, :], bcd[:], ALU.mult),
                            reads=[bkey, ("bcd",)], writes=[("yn",)])
                        yield
                        for i4 in range(4):
                            tb, tkey = self.tbank()
                            p.op("T", lambda e, i4=i4, tb=tb: e.transpose(
                                tb[:, 0:64], yn[:, i4 * 128:(i4 + 1) * 128],
                                self.ident[0:64, 0:64]),
                                reads=[("yn",), ("ident",)], writes=[tkey])
                            yield
                            tg = c * 4 + i4
                            p.op("V", lambda e, tg=tg, h=h, tb=tb: e.tensor_tensor(
                                rsum[:, tg * 1024 + h * 64:tg * 1024 + h * 64 + 64],
                                tb[:, 0:64],
                                self.r[:, tg * 1024 + h * 64:tg * 1024 + h * 64 + 64],
                                ALU.add),
                                reads=[tkey, ("r", tg)], writes=[("rs", tg)])
                            yield

            def run(gen):
                for _ in gen:
                    pass

            def interleave(main, filler, ratio=1):
                while True:
                    done_main = False
                    for _ in range(ratio):
                        try:
                            next(main)
                        except StopIteration:
                            done_main = True
                            break
                    if filler is not None:
                        try:
                            next(filler)
                        except StopIteration:
                            filler = None
                    if done_main:
                        break
                if filler is not None:
                    run(filler)

            NG = H // 2
            run(prep(0))
            for g in range(NG):
                interleave(c3(g), prep(g + 1) if g + 1 < NG else None, ratio=2)

            for t in range(NT):
                self.ln_tile(("rs", t), rsum[:, t * 1024:(t + 1) * 1024], sq, s1, s2,
                             self.r[:, t * 1024:(t + 1) * 1024], [("r", t)], 1024, EPS_LN)
            p.barrier()


    def wquant_pass1(self, w_dram, rows, cols, tag):
        p = self.p
        acc, onef = self.wacc, self.onef
        RCH, CCH = 128, 512
        nch, ncc = rows // RCH, cols // CCH
        ac = 0 if tag == "a" else 1
        pc = 2 if tag == "a" else 3
        p.op("V", lambda e: e.memset(onef[:], 1.0), writes=[("onef",)])
        first = True
        for i in range(nch):
            for j in range(ncc):
                ii = i * ncc + j
                wc, wk = self.wch[ii % 2], ("wchp", ii % 2)
                p.op("Q", lambda e, i=i, j=j, wc=wc: e.dma_start(
                    wc[:], w_dram[i * RCH:(i + 1) * RCH, j * CCH:(j + 1) * CCH]),
                    writes=[wk], dma=True)
                if first:
                    p.op("V", lambda e, wc=wc, ac=ac: e.reduce_sum(
                        acc[:, ac:ac + 1], wc[:], axis=AXX, apply_absolute_value=True),
                        reads=[wk], writes=[("wacc", ac)])
                    first = False
                else:
                    p.op("V", lambda e, wc=wc, pc=pc: e.reduce_sum(
                        acc[:, pc:pc + 1], wc[:], axis=AXX, apply_absolute_value=True),
                        reads=[wk], writes=[("wacc2", pc)])
                    p.op("V", lambda e, ac=ac, pc=pc: e.tensor_tensor(
                        acc[:, ac:ac + 1], acc[:, ac:ac + 1], acc[:, pc:pc + 1], ALU.add),
                        reads=[("wacc", ac), ("wacc2", pc)], writes=[("wacc", ac)])

    def wquant_pass1_fin(self, rows, cols, tag):
        p = self.p
        acc, onef = self.wacc, self.onef
        ac = 0 if tag == "a" else 1
        mccol = 2 if tag == "a" else 3
        bank, bkey = self.bank()
        p.op("T", lambda e, bank=bank, ac=ac: e.matmul(
            bank[0:1, 0:1], acc[:, ac:ac + 1], onef[:], start=True, stop=True),
             reads=[("wacc", ac), ("onef",)], writes=[bkey])
        p.op("V", lambda e, bank=bank, mccol=mccol: e.tensor_scalar(
            acc[0:1, mccol:mccol + 1], bank[0:1, 0:1], 1.0 / (rows * cols), 1e-5,
            ALU.mult, ALU.max),
            reads=[bkey], writes=[("wmc", tag), ("wacc2", mccol)])
        bank2, bkey2 = self.bank()
        p.op("T", lambda e, bank2=bank2, mccol=mccol: e.matmul(
            bank2[:, 0:1], self.onerow[:], acc[0:1, mccol:mccol + 1], start=True, stop=True),
            reads=[("wmc", tag), ("onerow",)], writes=[bkey2])
        p.op("V", lambda e, bank2=bank2, tag=tag: e.tensor_copy(self.swi[tag][:],
                                                                bank2[:, 0:1]),
             reads=[bkey2], writes=[(tag, "swinv")])
        p.op("V", lambda e, tag=tag: e.reciprocal(self.swv[tag][:], self.swi[tag][:]),
             reads=[(tag, "swinv")], writes=[(tag, "sw")])

    def wquant_pass2(self, w_dram, rows, cols, wqT, tag, store):
        nc, p = self.nc, self.p
        sw = self.swv[tag]
        RCH, CCH = 128, 512
        nch, ncc = rows // RCH, cols // CCH
        for i in range(nch):
            for j in range(ncc):
                ii = i * ncc + j
                wc, wk = self.wch[ii % 2], ("wchp", ii % 2)
                q = "Q" if ii % 2 == 0 else "P"
                p.op(q, lambda e, i=i, j=j, wc=wc: e.dma_start(
                    wc[:], w_dram[i * RCH:(i + 1) * RCH, j * CCH:(j + 1) * CCH]),
                    writes=[wk], dma=True)
                p.op("V", lambda e, wc=wc: e.tensor_scalar(
                    wc[:], wc[:], sw[:], None, ALU.mult),
                    reads=[wk, (tag, "sw")], writes=[wk])
                p.op("V", lambda e, wc=wc: e.tensor_scalar(
                    wc[:], wc[:], self.magic[:], self.magic[:], ALU.add, ALU.subtract),
                    reads=[wk, ("magic",)], writes=[wk])
                p.op("V", lambda e, wc=wc: e.tensor_scalar(
                    wc[:], wc[:], 1.0, -1.0, ALU.min, ALU.max),
                    reads=[wk], writes=[wk])
                for cl in range(CCH // 128):
                    ct = j * (CCH // 128) + cl
                    bank, bkey = self.bank()
                    p.op("T", lambda e, cl=cl, wc=wc, bank=bank: e.transpose(
                        bank[:, 0:128], wc[:, cl * 128:(cl + 1) * 128], self.identf[:]),
                        reads=[wk, ("identf",)], writes=[bkey])
                    dst = wqT[:, ct * rows + i * RCH:ct * rows + (i + 1) * RCH]
                    if ct % 2 == 0:
                        p.op("V", lambda e, dst=dst, bank=bank: e.tensor_copy(
                            dst, bank[:, 0:128]),
                            reads=[bkey], writes=[(tag, "wqT")])
                    else:
                        p.op("S", lambda e, dst=dst, bank=bank: e.copy(
                            dst, bank[:, 0:128]),
                            reads=[bkey], writes=[(tag, "wqT")])
        if store is not None:
            p.op("P", lambda e: e.dma_start(store.ap(), wqT[:]),
                 reads=[(tag, "wqT")], writes=[(tag, "wq_dram")], dma=True)

    def wquant_load(self, wqT, tag, store):
        p = self.p
        n = 8
        step = wqT.shape[1] // n
        for i in range(n):
            q = "Q" if i % 2 == 0 else "P"
            p.op(q, lambda e, i=i: e.dma_start(
                wqT[:, i * step:(i + 1) * step], store.ap()[:, i * step:(i + 1) * step]),
                reads=[(tag, "wq_dram")], writes=[(tag, "wqT")], dma=True)

    def l1_ffn(self, norm1_w, w1, act_scale, act_bias, norm2_w, w2, out_dram=None, out_s=0,
               first_seq=True):
        nc, p = self.nc, self.p
        S, NT = self.S, self.NT
        FH = FFN_H
        HT = FH // 128
        if "a" not in self.wq_dram:
            self.wq_dram["a"] = nc.dram_tensor("wqs_a", [128, DT * FH], BF16)
            self.wq_dram["b"] = nc.dram_tensor("wqs_b", [128, HT * 1024], BF16)
        with contextlib.ExitStack() as st:
            sb = lambda name, shape, dt: self._sb(st, name, shape, dt)
            w1qT = sb("w1qT", [128, DT * FH], BF16)
            w2qT = sb("w2qT", [128, HT * 1024], BF16)
            for c in range(2):
                p.op("P", lambda e, c=c: e.dma_start(
                    nrow[0:1, :], norm1_w[None, c * 512:(c + 1) * 512]),
                    writes=[("nrow",)], dma=True)
                bank, bkey = self.bank()
                p.op("T", lambda e, bank=bank: e.matmul(
                    bank[:], self.ones_bf[0:1, 0:128], nrow[0:1, :],
                    start=True, stop=True),
                    reads=[("nrow",), ("ones_bf",)], writes=[bkey])
                p.op("V", lambda e, c=c, bank=bank: e.tensor_copy(
                    n1b[:, c * 512:(c + 1) * 512], bank[:]),
                    reads=[bkey], writes=[("n1b",)])
            for c in range(FH // 512):
                p.op("P", lambda e, c=c: e.dma_start(
                    nrow[0:1, :], norm2_w[None, c * 512:(c + 1) * 512]),
                    writes=[("nrow",)], dma=True)
                bank, bkey = self.bank()
                p.op("T", lambda e, bank=bank: e.matmul(
                    bank[:], self.ones_bf[0:1, 0:128], nrow[0:1, :],
                    start=True, stop=True),
                    reads=[("nrow",), ("ones_bf",)], writes=[bkey])
                p.op("V", lambda e, c=c, bank=bank: e.tensor_copy(
                    n2b[:, c * 512:(c + 1) * 512], bank[:]),
                    reads=[bkey], writes=[("n2b",)])
            p.op("P", lambda e: e.dma_start(scal[0:1, 0:1], act_scale[None, :]),
                 writes=[("sc0",)], dma=True)
            p.op("P", lambda e: e.dma_start(scal[0:1, 1:2], act_bias[None, :]),
                 writes=[("sc1",)], dma=True)
            bank, bkey = self.bank()
            p.op("T", lambda e, bank=bank: e.matmul(
                bank[:, 0:2], self.onerow[:], scal[0:1, 0:2], start=True, stop=True),
                reads=[("sc0",), ("sc1",), ("onerow",)], writes=[bkey])
            p.op("V", lambda e, bank=bank: e.tensor_copy(asb[:], bank[:, 0:2]),
                 reads=[bkey], writes=[("asb",)])

            sw1i, sw2i = self.swi["a"], self.swi["b"]
            if first_seq:
                self.wquant_pass1_fin(FH, 1024, "a")
                self.wquant_pass1_fin(1024, FH, "b")
                self.wquant_pass2(w1, FH, 1024, w1qT, "a", self.wq_dram["a"])
                self.wquant_pass2(w2, 1024, FH, w2qT, "b", self.wq_dram["b"])
            else:
                self.wquant_load(w1qT, "a", self.wq_dram["a"])
                self.wquant_load(w2qT, "b", self.wq_dram["b"])
            n1b = sb("n1b", [128, 1024], BF16)
            n2b = sb("n2b", [128, FH], BF16)
            asb = sb("asb", [128, 2], F32)
            nrow = sb("nrow", [1, 512], BF16)
            scal = sb("scal", [128, 8], F32)
            xn = sb("xn", [128, 1024], F32)
            xq = sb("xq", [128, 1024], BF16)
            xqT = sb("xqT", [128, DT * 128], BF16)
            h2 = sb("h2", [128, FH], BF16)
            xq2 = h2
            xq2T = sb("xq2T", [128, HT * 128], BF16)
            tc1 = xn[:, 0:512]
            tc2 = xn[:, 512:1024]
            rsum = sb("rsum", [128, 1024], F32)
            s1 = sb("s1", [128, 4], F32)
            s2 = sb("s2", [128, 4], F32)
            rs = sb("rs", [128, 8], F32)

            for t in range(NT):
                rk = ("r", t)
                rt = self.r[:, t * 1024:(t + 1) * 1024]
                p.op("S", lambda e, rt=rt: e.activation(xn[:], rt, AF.Square),
                     reads=[rk], writes=[("xn",)])
                p.op("V", lambda e: e.reduce_sum(rs[:, 0:1], xn[:], axis=AXX),
                     reads=[("xn",)], writes=[("s1a",)])
                p.op("V", lambda e: e.tensor_scalar(rs[:, 1:2], rs[:, 0:1], 1.0 / 1024,
                                                    EPS_NORM, ALU.mult, ALU.add),
                     reads=[("s1a",)], writes=[("s1b",)])
                p.op("S", lambda e: e.activation(rs[:, 2:3], rs[:, 1:2], AF.Sqrt),
                     reads=[("s1b",)], writes=[("s1c",)])
                p.op("V", lambda e: e.reciprocal(rs[:, 3:4], rs[:, 2:3]),
                     reads=[("s1c",)], writes=[("s1d",)])
                p.op("V", lambda e, rt=rt: e.tensor_scalar(xn[:], rt, rs[:, 3:4], None,
                                                           ALU.mult),
                     reads=[rk, ("s1d",)], writes=[("xn",)])
                p.op("V", lambda e: e.tensor_tensor(xn[:], xn[:], n1b[:], ALU.mult),
                     reads=[("xn",), ("n1b",)], writes=[("xn",)])
                p.op("V", lambda e: e.reduce_max(scal[:, 4:5], xn[:], axis=AXX,
                                                 apply_absolute_value=True),
                     reads=[("xn",)], writes=[("am1",)])
                p.op("V", lambda e: e.tensor_scalar(scal[:, 5:6], scal[:, 4:5], 1e-5,
                                                    1.0 / 127.0, ALU.max, ALU.mult),
                     reads=[("am1",)], writes=[("sxi",)])
                p.op("V", lambda e: e.reciprocal(scal[:, 6:7], scal[:, 5:6]),
                     reads=[("sxi",)], writes=[("sx",)])
                p.op("V", lambda e: e.tensor_scalar(xn[:], xn[:], scal[:, 6:7], None,
                                                    ALU.mult),
                     reads=[("xn",), ("sx",)], writes=[("xn",)])
                p.op("V", lambda e: e.tensor_scalar(xn[:], xn[:], self.magic[:],
                                                    self.magic[:], ALU.add, ALU.subtract),
                     reads=[("xn",), ("magic",)], writes=[("xn",)])
                p.op("V", lambda e: e.tensor_scalar(xq[:], xn[:], 127.0, -128.0,
                                                    ALU.min, ALU.max),
                     reads=[("xn",)], writes=[("xq",)])
                for dt in range(DT):
                    tb, tkey = self.tbank()
                    p.op("T", lambda e, dt=dt, tb=tb: e.transpose(
                        tb[:, 0:128], xq[:, dt * 128:(dt + 1) * 128], self.ident[:]),
                        reads=[("xq",), ("ident",)], writes=[tkey])
                    p.op("V", lambda e, dt=dt, tb=tb: e.tensor_copy(
                        xqT[:, dt * 128:(dt + 1) * 128], tb[:, 0:128]),
                        reads=[tkey], writes=[("xqT", dt)])
                p.op("V", lambda e: e.tensor_tensor(scal[:, 7:8], scal[:, 5:6], sw1i[:],
                                                    ALU.mult),
                     reads=[("sxi",), ("a", "swinv")], writes=[("hsc",)])
                p.op("V", lambda e: e.tensor_tensor(scal[:, 7:8], scal[:, 7:8], scal[:, 7:8],
                                                    ALU.mult),
                     reads=[("hsc",)], writes=[("hsc",)])
                p.op("V", lambda e: e.tensor_scalar(scal[:, 7:8], scal[:, 7:8], asb[:, 0:1],
                                                    None, ALU.mult),
                     reads=[("hsc",), ("asb",)], writes=[("hsc",)])
                for ch in range(FH // 512):
                    bank, bkey = self.bank()

                    def mm(e, ch=ch, bank=bank):
                        for dt in range(DT):
                            ins = e.matmul(
                                bank[:],
                                xqT[:, dt * 128:(dt + 1) * 128],
                                w1qT[:, dt * FH + ch * 512:dt * FH + (ch + 1) * 512],
                                start=(dt == 0), stop=(dt == DT - 1))
                        return ins
                    p.op("T", mm, reads=[("xqT", dt2) for dt2 in range(DT)]
                         + [("a", "wqT")],
                         writes=[bkey])
                    p.op("S", lambda e, bank=bank: e.activation(tc1, bank[:], AF.Relu),
                         reads=[bkey], writes=[("xn",)])
                    p.op("S", lambda e: e.activation(tc2, tc1, AF.Square),
                         reads=[("xn",)], writes=[("xn",)])
                    p.op("V", lambda e, ch=ch: e.tensor_scalar(
                        h2[:, ch * 512:(ch + 1) * 512], tc2, scal[:, 7:8], asb[:, 1:2],
                        ALU.mult, ALU.add),
                        reads=[("xn",), ("hsc",), ("asb",)], writes=[("h2", ch)])
                for ch in range(FH // 512):
                    p.op("S", lambda e, ch=ch: e.activation(
                        tc1, h2[:, ch * 512:(ch + 1) * 512], AF.Square),
                        reads=[("h2", ch)], writes=[("xn",)])
                    if ch == 0:
                        p.op("V", lambda e: e.reduce_sum(rs[:, 4:5], tc1, axis=AXX),
                             reads=[("xn",)], writes=[("s2a",)])
                    else:
                        p.op("V", lambda e: e.reduce_sum(rs[:, 5:6], tc1, axis=AXX),
                             reads=[("xn",)], writes=[("s2b",)])
                        p.op("V", lambda e: e.tensor_tensor(rs[:, 4:5], rs[:, 4:5],
                                                            rs[:, 5:6], ALU.add),
                             reads=[("s2a",), ("s2b",)], writes=[("s2a",)])
                p.op("V", lambda e: e.tensor_scalar(rs[:, 4:5], rs[:, 4:5], 1.0 / FH,
                                                    EPS_NORM, ALU.mult, ALU.add),
                     reads=[("s2a",)], writes=[("s2a",)])
                p.op("S", lambda e: e.activation(rs[:, 6:7], rs[:, 4:5], AF.Sqrt),
                     reads=[("s2a",)], writes=[("s2c",)])
                p.op("V", lambda e: e.reciprocal(rs[:, 7:8], rs[:, 6:7]),
                     reads=[("s2c",)], writes=[("s2d",)])
                for ch in range(FH // 512):
                    hc = h2[:, ch * 512:(ch + 1) * 512]
                    p.op("V", lambda e, hc=hc, ch=ch: e.tensor_tensor(
                        hc, hc, n2b[:, ch * 512:(ch + 1) * 512], ALU.mult),
                        reads=[("h2", ch), ("n2b",)], writes=[("h2", ch)])
                    if ch == 0:
                        p.op("V", lambda e, hc=hc: e.reduce_max(
                            scal[:, 4:5], hc, axis=AXX, apply_absolute_value=True),
                             reads=[("h2", ch)], writes=[("am1",)])
                    else:
                        p.op("V", lambda e, hc=hc: e.reduce_max(
                            scal[:, 3:4], hc, axis=AXX, apply_absolute_value=True),
                             reads=[("h2", ch)], writes=[("amt",)])
                        p.op("V", lambda e: e.tensor_tensor(scal[:, 4:5], scal[:, 4:5],
                                                            scal[:, 3:4], ALU.max),
                             reads=[("am1",), ("amt",)], writes=[("am1",)])
                p.op("V", lambda e: e.tensor_tensor(scal[:, 4:5], scal[:, 4:5], rs[:, 7:8],
                                                    ALU.mult),
                     reads=[("am1",), ("s2d",)], writes=[("am1",)])
                p.op("V", lambda e: e.tensor_scalar(scal[:, 5:6], scal[:, 4:5], 1e-5,
                                                    1.0 / 127.0, ALU.max, ALU.mult),
                     reads=[("am1",)], writes=[("sxi",)])
                p.op("V", lambda e: e.reciprocal(scal[:, 6:7], scal[:, 5:6]),
                     reads=[("sxi",)], writes=[("sx",)])
                p.op("V", lambda e: e.tensor_tensor(scal[:, 6:7], scal[:, 6:7], rs[:, 7:8],
                                                    ALU.mult),
                     reads=[("sx",), ("s2d",)], writes=[("sx",)])
                for ch in range(FH // 512):
                    p.op("V", lambda e, ch=ch: e.tensor_scalar(
                        tc1, h2[:, ch * 512:(ch + 1) * 512], scal[:, 6:7], None,
                        ALU.mult),
                        reads=[("h2", ch), ("sx",)], writes=[("xn",)])
                    p.op("V", lambda e: e.tensor_scalar(tc1, tc1, self.magic[:],
                                                        self.magic[:], ALU.add, ALU.subtract),
                         reads=[("xn",), ("magic",)], writes=[("xn",)])
                    p.op("V", lambda e, ch=ch: e.tensor_scalar(
                        xq2[:, ch * 512:(ch + 1) * 512], tc1, 127.0, -128.0,
                        ALU.min, ALU.max),
                        reads=[("xn",)], writes=[("h2", ch)])
                for ht in range(HT):
                    tb, tkey = self.tbank()
                    p.op("T", lambda e, ht=ht, tb=tb: e.transpose(
                        tb[:, 0:128], xq2[:, ht * 128:(ht + 1) * 128], self.ident[:]),
                        reads=[("h2", ht // 4), ("ident",)], writes=[tkey])
                    if ht % 2 == 0:
                        p.op("V", lambda e, ht=ht, tb=tb: e.tensor_copy(
                            xq2T[:, ht * 128:(ht + 1) * 128], tb[:, 0:128]),
                            reads=[tkey], writes=[("xq2T", ht)])
                    else:
                        p.op("S", lambda e, ht=ht, tb=tb: e.copy(
                            xq2T[:, ht * 128:(ht + 1) * 128], tb[:, 0:128]),
                            reads=[tkey], writes=[("xq2T", ht)])
                p.op("V", lambda e: e.tensor_tensor(scal[:, 7:8], scal[:, 5:6], sw2i[:],
                                                    ALU.mult),
                     reads=[("sxi",), ("b", "swinv")], writes=[("hsc2",)])
                for ch in range(2):
                    bank, bkey = self.bank()

                    def mm(e, ch=ch, bank=bank):
                        for ht in range(HT):
                            ins = e.matmul(
                                bank[:],
                                xq2T[:, ht * 128:(ht + 1) * 128],
                                w2qT[:, ht * 1024 + ch * 512:ht * 1024 + (ch + 1) * 512],
                                start=(ht == 0), stop=(ht == HT - 1))
                        return ins
                    p.op("T", mm, reads=[("xq2T", ht2) for ht2 in range(HT)]
                         + [("b", "wqT")],
                         writes=[bkey])
                    p.op("V", lambda e, ch=ch, rt=rt, bank=bank: e.scalar_tensor_tensor(
                        rsum[:, ch * 512:(ch + 1) * 512], bank[:], scal[:, 7:8],
                        rt[:, ch * 512:(ch + 1) * 512], ALU.mult, ALU.add),
                        reads=[bkey, ("hsc2",), rk], writes=[("rsum",)])
                if out_dram is None:
                    self.ln_tile(("rsum",), rsum[:], xn, s1, s2, rt, [rk], 1024, EPS_LN,
                                 kq=("xn",))
                else:
                    self.ln_tile(("rsum",), rsum[:], xn, s1, s2, rsum[:], [("rsum",)],
                                 1024, EPS_LN, kq=("xn",))
                    dst = out_dram[out_s, t * 128:(t + 1) * 128, :]
                    p.op("Q", lambda e, dst=dst: e.dma_start(dst, rsum[:]),
                         reads=[("rsum",)], dma=True)
            p.barrier()

def build_l0_attn_program(S):
    nc = bass.Bass("TRN2", target_bir_lowering=False)
    x = nc.dram_tensor("x", [1, S, 1024], F32, kind="ExternalInput").ap()
    w_q = nc.dram_tensor("w_q", [1024, 1024], F32, kind="ExternalInput").ap()
    w_kv = nc.dram_tensor("w_kv", [1024, 1024], F32, kind="ExternalInput").ap()
    w_o = nc.dram_tensor("w_o", [1024, 1024], F32, kind="ExternalInput").ap()
    k_w = nc.dram_tensor("k_w", [1024], F32, kind="ExternalInput").ap()
    out = nc.dram_tensor("out", [1, S, 1024], F32, kind="ExternalOutput").ap()
    with contextlib.ExitStack() as stack:
        prog = Prog(nc, stack)
        k = Kern(nc, prog, stack, S)
        k.init_consts()
        k.load_r(x, 0)
        k.l0_attn(w_q, w_kv, w_o, k_w)
        k.store_r(out, 0)
        prog.drain()
        prog.finalize()
    return nc


def build_l0_ffn_program(S):
    nc = bass.Bass("TRN2", target_bir_lowering=False)
    x = nc.dram_tensor("x", [1, S, 1024], F32, kind="ExternalInput").ap()
    rw = nc.dram_tensor("rw", [8, 1024], F32, kind="ExternalInput").ap()
    w1 = nc.dram_tensor("w1", [1024, 8192], F32, kind="ExternalInput").ap()
    w2 = nc.dram_tensor("w2", [8192, 1024], F32, kind="ExternalInput").ap()
    out = nc.dram_tensor("out", [1, S, 1024], F32, kind="ExternalOutput").ap()
    with contextlib.ExitStack() as stack:
        prog = Prog(nc, stack)
        k = Kern(nc, prog, stack, S)
        k.init_consts()
        k.load_r(x, 0)
        k.l0_ffn(rw, w1, w2)
        k.store_r(out, 0)
        prog.drain()
        prog.finalize()
    return nc


def build_l1_attn_program(S):
    nc = bass.Bass("TRN2", target_bir_lowering=False)
    x = nc.dram_tensor("x", [1, S, 1024], F32, kind="ExternalInput").ap()
    Wq = nc.dram_tensor("Wq", [1024, 1024], F32, kind="ExternalInput").ap()
    Wk = nc.dram_tensor("Wk", [64, 1024], F32, kind="ExternalInput").ap()
    Wv = nc.dram_tensor("Wv", [64, 1024], F32, kind="ExternalInput").ap()
    out = nc.dram_tensor("out", [1, S, 1024], F32, kind="ExternalOutput").ap()
    with contextlib.ExitStack() as stack:
        prog = Prog(nc, stack)
        k = Kern(nc, prog, stack, S)
        k.init_consts()
        k.load_r(x, 0)
        k.l1_attn(Wq, Wk, Wv)
        k.store_r(out, 0)
        prog.drain()
        prog.finalize()
    return nc


def build_l1_ffn_program(S):
    nc = bass.Bass("TRN2", target_bir_lowering=False)
    x = nc.dram_tensor("x", [1, S, 1024], F32, kind="ExternalInput").ap()
    n1 = nc.dram_tensor("n1", [1024], F32, kind="ExternalInput").ap()
    w1 = nc.dram_tensor("w1", [4096, 1024], F32, kind="ExternalInput").ap()
    asc = nc.dram_tensor("asc", [1], F32, kind="ExternalInput").ap()
    abi = nc.dram_tensor("abi", [1], F32, kind="ExternalInput").ap()
    n2 = nc.dram_tensor("n2", [4096], F32, kind="ExternalInput").ap()
    w2 = nc.dram_tensor("w2", [1024, 4096], F32, kind="ExternalInput").ap()
    out = nc.dram_tensor("out", [1, S, 1024], F32, kind="ExternalOutput").ap()
    with contextlib.ExitStack() as stack:
        prog = Prog(nc, stack)
        k = Kern(nc, prog, stack, S)
        k.init_consts()
        k.wquant_pass1(w1, FFN_H, 1024, "a")
        k.wquant_pass1(w2, 1024, FFN_H, "b")
        k.load_r(x, 0)
        k.l1_ffn(n1, w1, asc, abi, n2, w2, out_dram=out, out_s=0)
        prog.drain()
        prog.finalize()
    return nc


_STAGE_WEIGHTS = {
    "l0a": [("w_q", [1024, 1024]), ("w_kv", [1024, 1024]), ("w_o", [1024, 1024]),
            ("k_w", [1024])],
    "l0f": [("rw", [8, 1024]), ("w1", [1024, 8192]), ("w2", [8192, 1024])],
    "l1a": [("Wq", [1024, 1024]), ("Wk", [64, 1024]), ("Wv", [64, 1024])],
    "l1f": [("n1", [1024]), ("w1", [4096, 1024]), ("asc", [1]), ("abi", [1]),
            ("n2", [4096]), ("w2", [1024, 4096])],
}


def build_stage(stage, S, B):
    nc = bass.Bass("TRN2", target_bir_lowering=False)
    x = nc.dram_tensor("x", [B, S, 1024], F32, kind="ExternalInput").ap()
    w = {name: nc.dram_tensor(name, shape, F32, kind="ExternalInput").ap()
         for name, shape in _STAGE_WEIGHTS[stage]}
    out = nc.dram_tensor("out", [B, S, 1024], F32, kind="ExternalOutput").ap()
    with contextlib.ExitStack() as stack:
        prog = Prog(nc, stack)
        k = Kern(nc, prog, stack, S)
        k.init_consts()
        for s in range(B):
            k.load_r(x, s)
            if stage == "l0a":
                k.l0_attn(w["w_q"], w["w_kv"], w["w_o"], w["k_w"])
                k.store_r(out, s)
            elif stage == "l0f":
                k.l0_ffn(w["rw"], w["w1"], w["w2"])
                k.store_r(out, s)
            elif stage == "l1a":
                k.l1_attn(w["Wq"], w["Wk"], w["Wv"])
                k.store_r(out, s)
            else:
                if s == 0:
                    k.wquant_pass1(w["w1"], FFN_H, 1024, "a")
                    k.wquant_pass1(w["w2"], 1024, FFN_H, "b")
                k.l1_ffn(w["n1"], w["w1"], w["asc"], w["abi"], w["n2"], w["w2"],
                         out_dram=out, out_s=s, first_seq=(s == 0))
        prog.drain()
        prog.finalize()
    return nc


def build_fused(S, B):
    nc = bass.Bass("TRN2", target_bir_lowering=False)
    x = nc.dram_tensor("x", [B, S, 1024], F32, kind="ExternalInput").ap()
    w = {}
    for stage, lst in _STAGE_WEIGHTS.items():
        for name, shape in lst:
            w[stage + "_" + name] = nc.dram_tensor(
                stage + "_" + name, shape, F32, kind="ExternalInput").ap()
    out = nc.dram_tensor("out", [B, S, 1024], F32, kind="ExternalOutput").ap()
    with contextlib.ExitStack() as stack:
        prog = Prog(nc, stack)
        k = Kern(nc, prog, stack, S)
        k.init_consts()
        for s in range(B):
            k.load_r(x, s)
            if s == 0:
                k.wquant_pass1(w["l1f_w1"], FFN_H, 1024, "a")
                k.wquant_pass1(w["l1f_w2"], 1024, FFN_H, "b")
            k.l0_attn(w["l0a_w_q"], w["l0a_w_kv"], w["l0a_w_o"], w["l0a_k_w"])
            k.l0_ffn(w["l0f_rw"], w["l0f_w1"], w["l0f_w2"])
            k.l1_attn(w["l1a_Wq"], w["l1a_Wk"], w["l1a_Wv"])
            k.l1_ffn(w["l1f_n1"], w["l1f_w1"], w["l1f_asc"], w["l1f_abi"],
                     w["l1f_n2"], w["l1f_w2"], out_dram=out, out_s=s,
                     first_seq=(s == 0))
        prog.drain()
        prog.finalize()
    return nc


_PROG_CACHE = {}


def _stage_prog(stage, S, B):
    key = (stage, S, B)
    if key not in _PROG_CACHE:
        _PROG_CACHE[key] = build_stage(stage, S, B)
    return _PROG_CACHE[key]


def kernel(**inputs):
    S, B_PER = 2048, 2
    x = np.ascontiguousarray(inputs["L0_attn_x"], dtype=np.float32)
    f32 = lambda a: np.ascontiguousarray(a, dtype=np.float32)
    shards = [x[c * B_PER:(c + 1) * B_PER] for c in range(NCORES)]
    wmap = {
        "l0a_w_q": f32(inputs["L0_attn_w_q"]), "l0a_w_kv": f32(inputs["L0_attn_w_kv"]),
        "l0a_w_o": f32(inputs["L0_attn_w_o"]),
        "l0a_k_w": f32(inputs["L0_attn_k_weights"]).reshape(1024),
        "l0f_rw": f32(inputs["L0_ffn_router_w"]), "l0f_w1": f32(inputs["L0_ffn_w1"]),
        "l0f_w2": f32(inputs["L0_ffn_w2"]),
        "l1a_Wq": f32(inputs["L1_attn_Wq"]), "l1a_Wk": f32(inputs["L1_attn_Wk"]),
        "l1a_Wv": f32(inputs["L1_attn_Wv"]),
        "l1f_n1": f32(inputs["L1_ffn_norm1_w"]), "l1f_w1": f32(inputs["L1_ffn_w1"]),
        "l1f_asc": f32(inputs["L1_ffn_act_scale"]), "l1f_abi": f32(inputs["L1_ffn_act_bias"]),
        "l1f_n2": f32(inputs["L1_ffn_norm2_w"]), "l1f_w2": f32(inputs["L1_ffn_w2"]),
    }
    nc = build_fused(S, B_PER)
    in_maps = [dict(wmap, x=shards[c]) for c in range(NCORES)]
    res = run_bass_kernel_spmd(nc, in_maps, core_ids=list(range(NCORES)))
    return np.concatenate([np.asarray(res.results[c]["out"]) for c in range(NCORES)],
                          axis=0)
